# version 11
# speedup vs baseline: 1.3717x; 1.0255x over previous
import math
import os
import contextlib
import numpy as np

import concourse.bass as bass
import concourse.mybir as mybir
from concourse.bass_utils import run_bass_kernel_spmd

F32 = mybir.dt.float32
BF16 = mybir.dt.bfloat16
AF = mybir.ActivationFunctionType
ALU = mybir.AluOpType
AX = mybir.AxisListType

D = 4096
H = 32
DH = 128
S = 8192
NB = 2
NCORE = 8
CORE_T = 2048
HALO = 256
SH_R = CORE_T + 2 * HALO
TQ = 512
NT = CORE_T // TQ
NIC = D // 128
SCALE = 1.0 / math.sqrt(DH)
EPS = 1e-5
NE, RR, KR = 8, 64, 128
ROUTE_C = 1.0 / (S * math.sqrt(KR))
NKC = SH_R // 128
MASK_NEG = -1e9
STAGE = int(os.environ.get("KSTAGE", "9"))
KSUB = os.environ.get("KSUB", "all")


def build_program():
    nc = bass.Bass("TRN2", target_bir_lowering=False)

    xs = nc.dram_tensor("xs", [SH_R, D], F32, kind="ExternalInput").ap()
    wq = nc.dram_tensor("wq", [H, NIC, 128, 128], F32, kind="ExternalInput").ap()
    wo = nc.dram_tensor("wo", [8, NIC, 128, 512], F32, kind="ExternalInput").ap()
    worg = nc.dram_tensor("worg", [8, NIC, 128, 512], F32, kind="ExternalInput").ap()
    wkv = nc.dram_tensor("wkv", [NIC, 128, 256], F32, kind="ExternalInput").ap()
    ldn = nc.dram_tensor("ldn", [4, NIC, 128, 128], F32, kind="ExternalInput").ap()
    lup = nc.dram_tensor("lup", [8, 4, 128, 512], F32, kind="ExternalInput").ap()
    wqf = nc.dram_tensor("wqf", [KR, D], F32, kind="ExternalInput").ap()
    keysT = nc.dram_tensor("keysT", [KR, NE], F32, kind="ExternalInput").ap()
    cosT = nc.dram_tensor("cosT", [128, SH_R], F32, kind="ExternalInput").ap()
    sinT = nc.dram_tensor("sinT", [128, SH_R], F32, kind="ExternalInput").ap()
    protI = nc.dram_tensor("protI", [128, 128], F32, kind="ExternalInput").ap()
    identf = nc.dram_tensor("identf", [128, 128], F32, kind="ExternalInput").ap()
    alpha_c = nc.dram_tensor("alpha_c", [128, 16], F32, kind="ExternalInput").ap()
    alpham_c = nc.dram_tensor("alpham_c", [128, 16], F32, kind="ExternalInput").ap()
    mask_r = nc.dram_tensor("mask_r", [1, 512], F32, kind="ExternalInput").ap()
    ones_r = nc.dram_tensor("ones_r", [1, 512], F32, kind="ExternalInput").ap()
    bmaskT = nc.dram_tensor("bmaskT", [1, NB], F32, kind="ExternalInput").ap()
    bmaskP = nc.dram_tensor("bmaskP", [NB, 1], F32, kind="ExternalInput").ap()
    constc = nc.dram_tensor("constc", [128, 4], F32, kind="ExternalInput").ap()

    out = nc.dram_tensor("out", [CORE_T, D], F32, kind="ExternalOutput").ap()

    xT1h_sc = nc.dram_tensor("xT1h_sc", [NT, NIC, 128, TQ], BF16)
    xT1l_sc = nc.dram_tensor("xT1l_sc", [NT, NIC, 128, TQ], BF16)
    res1_sc = nc.dram_tensor("res1_sc", [16, 128, D], F32)
    x2Th_sc = nc.dram_tensor("x2Th_sc", [NT, NIC, 128, TQ], BF16)
    x2Tl_sc = nc.dram_tensor("x2Tl_sc", [NT, NIC, 128, TQ], BF16)
    ffn_sc = nc.dram_tensor("ffn_sc", [16, 128, D], F32)
    cc_in = nc.dram_tensor("cc_in", [NB, NE], F32)
    cc_out = nc.dram_tensor("cc_out", [NB, NE], F32)
    wqh_sc = nc.dram_tensor("wqh_sc", [H, NIC, 128, 128], BF16)
    wql_sc = nc.dram_tensor("wql_sc", [H, NIC, 128, 128], BF16)
    woh_sc = nc.dram_tensor("woh_sc", [8, NIC, 128, 512], BF16)
    wol_sc = nc.dram_tensor("wol_sc", [8, NIC, 128, 512], BF16)
    wbh_sc = nc.dram_tensor("wbh_sc", [8, NIC, 128, 512], BF16)
    wbl_sc = nc.dram_tensor("wbl_sc", [8, NIC, 128, 512], BF16)
    ldh_sc = nc.dram_tensor("ldh_sc", [4, NIC, 128, 128], BF16)
    ldl_sc = nc.dram_tensor("ldl_sc", [4, NIC, 128, 128], BF16)
    luh_sc = nc.dram_tensor("luh_sc", [8, 4, 128, 512], BF16)
    lul_sc = nc.dram_tensor("lul_sc", [8, 4, 128, 512], BF16)

    es = contextlib.ExitStack()
    with es:
        sems, cnt = {}, {}
        for name in ("t", "v", "s", "g", "sd", "gd", "cc",
                     "pre", "kap", "wkv", "cs",
                     "wqa", "wqb", "woa", "wob", "wba", "wbb",
                     "lda", "ldb", "lua", "lub",
                     "wla", "wlb", "sp0", "sp1", "sp2", "sp3"):
            sems[name] = es.enter_context(nc.semaphore("sem_" + name))
            cnt[name] = 0

        block = es.enter_context(nc.Block())

        class EngProxy:
            def __init__(self, eng):
                object.__setattr__(self, "_e", eng)
                object.__setattr__(self, "last", None)

            def __getattr__(self, n):
                a = getattr(self._e, n)
                if callable(a) and n not in ("drain", "wait_ge"):
                    def w(*args, **kw):
                        r = a(*args, **kw)
                        if n not in ("nop",):
                            object.__setattr__(self, "last", r)
                        return r
                    return w
                return a

        def emit(engname, fn):
            getattr(block, engname)(lambda e: fn(EngProxy(e)))

        def inc(instr, name, n=1):
            instr.then_inc(sems[name], n)
            cnt[name] += n
            return cnt[name]

        def wge(eng, name, val):
            if val > 0:
                eng.wait_ge(sems[name], val)

        def gflush(g):
            g.wait_ge(sems["gd"], cnt["gd"])
            inc(g.engine_nop(), "g")
            return cnt["g"]

        krTh = es.enter_context(nc.sbuf_tensor("krTh", [128, SH_R], BF16))
        krTl = es.enter_context(nc.sbuf_tensor("krTl", [128, SH_R], BF16))
        vaugh = es.enter_context(nc.sbuf_tensor("vaugh", [128, NKC * 129], BF16))
        vaugl = es.enter_context(nc.sbuf_tensor("vaugl", [128, NKC * 129], BF16))
        protT = es.enter_context(nc.sbuf_tensor("protT", [128, 128], F32))
        idf = es.enter_context(nc.sbuf_tensor("idf", [128, 128], F32))
        alpt = es.enter_context(nc.sbuf_tensor("alpt", [128, 16], F32))
        alpmt = es.enter_context(nc.sbuf_tensor("alpmt", [128, 16], F32))
        maskt = es.enter_context(nc.sbuf_tensor("maskt", [1, 512], F32))
        onest = es.enter_context(nc.sbuf_tensor("onest", [1, 512], F32))
        bmt = es.enter_context(nc.sbuf_tensor("bmt", [1, NB], F32))
        bmp = es.enter_context(nc.sbuf_tensor("bmp", [NB, 1], F32))
        cct = es.enter_context(nc.sbuf_tensor("cct", [128, 4], F32))
        kaph = es.enter_context(nc.sbuf_tensor("kaph", [128, NIC * NE], BF16))
        kapl = es.enter_context(nc.sbuf_tensor("kapl", [128, NIC * NE], BF16))
        zbuf = es.enter_context(nc.sbuf_tensor("zbuf", [128, NIC * 16], F32))
        z32 = es.enter_context(nc.sbuf_tensor("z32", [128, NIC], F32))
        z32h = es.enter_context(nc.sbuf_tensor("z32h", [128, NIC], BF16))
        z32l = es.enter_context(nc.sbuf_tensor("z32l", [128, NIC], BF16))
        coefs = es.enter_context(nc.sbuf_tensor("coefs", [1, NE], F32))
        routs = es.enter_context(nc.sbuf_tensor("routs", [1, NE], F32))
        rout2 = es.enter_context(nc.sbuf_tensor("rout2", [NB, NE], F32))
        routp = es.enter_context(nc.sbuf_tensor("routp", [NB, NE], F32))
        bc8 = es.enter_context(nc.sbuf_tensor("bc8", [128, NE], F32))
        cer = es.enter_context(nc.sbuf_tensor("cer", [128, 4], F32))
        smalls = es.enter_context(nc.sbuf_tensor("smalls", [128, 72], F32))
        costt = es.enter_context(nc.sbuf_tensor("costt", [128, 512], F32))
        sintt = es.enter_context(nc.sbuf_tensor("sintt", [128, 512], F32))
        dumpb = es.enter_context(nc.sbuf_tensor("dumpb", [128, D], BF16))
        protB = es.enter_context(nc.sbuf_tensor("protB", [128, 128], BF16))
        maskB = es.enter_context(nc.sbuf_tensor("maskB", [1, 512], BF16))
        onesB = es.enter_context(nc.sbuf_tensor("onesB", [1, 512], BF16))
        bmtB = es.enter_context(nc.sbuf_tensor("bmtB", [1, NB], BF16))
        bmpB = es.enter_context(nc.sbuf_tensor("bmpB", [NB, 1], BF16))
        routsh = es.enter_context(nc.sbuf_tensor("routsh", [1, NE], BF16))
        routsl = es.enter_context(nc.sbuf_tensor("routsl", [1, NE], BF16))
        rout2h = es.enter_context(nc.sbuf_tensor("rout2h", [NB, NE], BF16))
        rout2l = es.enter_context(nc.sbuf_tensor("rout2l", [NB, NE], BF16))
        coefsh = es.enter_context(nc.sbuf_tensor("coefsh", [1, NE], BF16))
        coefsl = es.enter_context(nc.sbuf_tensor("coefsl", [1, NE], BF16))

        psA = es.enter_context(nc.psum_tensor("psA", [128, 1536], F32))
        psB = es.enter_context(nc.psum_tensor("psB", [128, 1536], F32))
        psC = es.enter_context(nc.psum_tensor("psC", [128, 512], F32))
        psD = es.enter_context(nc.psum_tensor("psD", [128, 512], F32))
        bank_list = [(psA, 0), (psA, 512), (psA, 1024), (psB, 0), (psB, 512), (psB, 1024), (psC, 0), (psD, 0)]

        def bank(i, w=512):
            tns, c = bank_list[i % 8]
            return tns[:, c:c + w]

        def mm3(tensor, out_ap, lh, ll, rh, rl, start, stop):
            tensor.matmul(out_ap, lh, rh, start=start, stop=False)
            tensor.matmul(out_ap, lh, rl, start=False, stop=False)
            tensor.matmul(out_ap, ll, rh, start=False, stop=stop)

        def _pre(sync):
            inc(sync.dma_start(out=protT[:], in_=protI[:]), "pre", 16)
            inc(sync.dma_start(out=idf[:], in_=identf[:]), "pre", 16)
            inc(sync.dma_start(out=alpt[:], in_=alpha_c[:]), "pre", 16)
            inc(sync.dma_start(out=alpmt[:], in_=alpham_c[:]), "pre", 16)
            inc(sync.dma_start(out=maskt[:], in_=mask_r[:]), "pre", 16)
            inc(sync.dma_start(out=onest[:], in_=ones_r[:]), "pre", 16)
            inc(sync.dma_start(out=bmt[:], in_=bmaskT[:]), "pre", 16)
            inc(sync.dma_start(out=bmp[:], in_=bmaskP[:]), "pre", 16)
            inc(sync.dma_start(out=cct[:], in_=constc[:]), "pre", 16)
        emit("sync", _pre)
        pre_sd = cnt["pre"]

        nc.const_aps.register(F32, 0.0, cct[:, 2:3])
        nc.const_aps.register(F32, 1.0, cct[:, 1:2])

        def _preb(scalar):
            wge(scalar, "pre", pre_sd)
            scalar.copy(protB[:], protT[:])
            scalar.drain()
            scalar.copy(maskB[:], maskt[:])
            scalar.drain()
            scalar.copy(onesB[:], onest[:])
            scalar.drain()
            scalar.copy(bmtB[:], bmt[:])
            scalar.drain()
            scalar.copy(bmpB[:], bmp[:])
            scalar.drain()
            inc(scalar.last, "s")
        emit("scalar", _preb)
        preb_s = cnt["s"]

        def split_hl(src_ap, hi_ap, lo_ap, waits_s=(), waits_v=()):
            def _v(vector):
                seen = set()
                for nm, vl in tuple(waits_s) + tuple(waits_v):
                    if (nm, vl) not in seen:
                        seen.add((nm, vl))
                        wge(vector, nm, vl)
                vector.tensor_scalar(hi_ap, src_ap, 1.0, None, ALU.mult)
                vector.drain()
                vector.tensor_tensor(lo_ap, src_ap, hi_ap, ALU.subtract)
                vector.drain()
                inc(vector.last, "v")
            emit("vector", _v)
            return cnt["s"], cnt["v"]

        st_kap = contextlib.ExitStack()
        kwq = st_kap.enter_context(nc.sbuf_tensor("kwq", [128, D], F32))
        kwqh = st_kap.enter_context(nc.sbuf_tensor("kwqh", [128, D], BF16))
        kwql = st_kap.enter_context(nc.sbuf_tensor("kwql", [128, D], BF16))
        kktf = st_kap.enter_context(nc.sbuf_tensor("kktf", [KR, NE], F32))
        kkth = st_kap.enter_context(nc.sbuf_tensor("kkth", [KR, NE], BF16))
        kktl = st_kap.enter_context(nc.sbuf_tensor("kktl", [KR, NE], BF16))
        kapf = st_kap.enter_context(nc.sbuf_tensor("kapf", [128, NIC * NE], F32))

        def _kap_ld(sync):
            inc(sync.dma_start(out=kwq[:], in_=wqf[:]), "kap", 16)
            inc(sync.dma_start(out=kktf[:], in_=keysT[:]), "kap", 16)
        emit("sync", _kap_ld)
        kap_sd = cnt["kap"]

        split_hl(kwq[:], kwqh[:], kwql[:], waits_s=(("kap", kap_sd),), waits_v=(("kap", kap_sd),))
        split_hl(kktf[:], kkth[:], kktl[:])
        kap_split_v = cnt["v"]
        kap_split_s = cnt["s"]

        def _kap_mm(tensor):
            wge(tensor, "v", kap_split_v)
            wge(tensor, "s", kap_split_s)
            nmm = NIC if KSUB == "all" else (0 if KSUB == "nomm" else int(KSUB))
            for ic in range(nmm):
                mm3(tensor, psA[:, ic * NE:(ic + 1) * NE],
                    kwqh[:, ic * 128:(ic + 1) * 128], kwql[:, ic * 128:(ic + 1) * 128],
                    kkth[:], kktl[:], True, True)
            inc(tensor.last, "t")
        emit("tensor", _kap_mm)
        kap_t = cnt["t"]

        def _kap_ev(vector):
            wge(vector, "t", kap_t)
            inc(vector.tensor_copy(kapf[:], psA[:, 0:NIC * NE]), "v")
            vector.drain()
        emit("vector", _kap_ev)
        split_hl(kapf[:], kaph[:], kapl[:], waits_s=(("v", cnt["v"]),))
        kap_v = cnt["v"]
        kap_s = cnt["s"]
        kap_t_f = cnt["t"]
        st_kap.close()

        def ln_chunk(buf, dst, base, waits=()):
            sm = smalls[:, base:base + 1]
            sq = smalls[:, base + 1:base + 2]
            m = smalls[:, base + 2:base + 3]
            var = smalls[:, base + 3:base + 4]

            def _s(scalar):
                for nm, vl in waits:
                    wge(scalar, nm, vl)
                scalar.activation(dumpb[:], buf, AF.Square, accum_out=sq)
                scalar.drain()
                inc(scalar.last, "s")
            emit("scalar", _s)
            sq_s = cnt["s"]

            def _v(vector):
                for nm, vl in waits:
                    wge(vector, nm, vl)
                vector.reduce_sum(sm, buf, axis=AX.X)
                vector.drain()
                wge(vector, "s", sq_s)
                vector.tensor_scalar(m, sm, 1.0 / D, None, ALU.mult)
                vector.drain()
                vector.tensor_scalar(smalls[:, 65:66], m, m, None, ALU.mult)
                vector.drain()
                vector.scalar_tensor_tensor(var, sq, 1.0 / D, smalls[:, 65:66], ALU.mult, ALU.subtract)
                vector.drain()
                inc(vector.last, "v")
            emit("vector", _v)
            var_v = cnt["v"]

            def _s2(scalar):
                wge(scalar, "v", var_v)
                scalar.activation(smalls[:, 66:67], var, AF.Ln, bias=cct[:, 0:1], scale=1.0)
                scalar.drain()
                scalar.activation(sq, smalls[:, 66:67], AF.Exp, scale=-0.5)
                scalar.drain()
                inc(scalar.last, "s")
            emit("scalar", _s2)
            rs_s = cnt["s"]

            def _v2(vector):
                wge(vector, "s", rs_s)
                vector.tensor_scalar(sm, m, sq, -1.0, ALU.mult, ALU.mult)
                vector.drain()
                vector.tensor_scalar(dst, buf, sq, sm, ALU.mult, ALU.add)
                vector.drain()
                inc(vector.last, "v")
            emit("vector", _v2)
            return cnt["v"]

        def transpose_split(srcs, dsts_h, dsts_l, first_waits=(), psum_off=0):
            ev_s = []
            ev_v = []
            for n, (sp, dh, dl) in enumerate(zip(srcs, dsts_h, dsts_l)):
                sl = bank(psum_off + n % 4, 128)

                def _t(tensor, sp=sp, sl=sl, n=n):
                    if n == 0:
                        for nm, vl in first_waits:
                            wge(tensor, nm, vl)
                    if n >= 4:
                        wge(tensor, "s", ev_s[n - 4])
                        wge(tensor, "v", ev_v[n - 4])
                    tensor.transpose(sl, sp, idf[:])
                    inc(tensor.last, "t")
                emit("tensor", _t)
                tmark = cnt["t"]

                def _v(vector, dl=dl, dh=dh, sl=sl, tmark=tmark):
                    wge(vector, "t", tmark)
                    vector.tensor_scalar(dh, sl, 1.0, None, ALU.mult)
                    vector.drain()
                    vector.tensor_tensor(dl, sl, dh, ALU.subtract)
                    vector.drain()
                    inc(vector.last, "v")
                emit("vector", _v)
                ev_s.append(cnt["s"])
                ev_v.append(cnt["v"])
            return cnt["s"], cnt["v"]

        if STAGE < 2:
            def _fini0(g):
                g.wait_ge(sems["gd"], cnt["gd"])
                g.wait_ge(sems["v"], cnt["v"])
                g.wait_ge(sems["s"], cnt["s"])
                g.wait_ge(sems["t"], cnt["t"])
            emit("gpsimd", _fini0)
            return nc
        p0 = contextlib.ExitStack()
        with p0:
            xsub = p0.enter_context(nc.sbuf_tensor("xsub", [128, 2 * D], F32))
            xT1sh = p0.enter_context(nc.sbuf_tensor("xT1sh", [128, NIC * 256], BF16))
            xT1sl = p0.enter_context(nc.sbuf_tensor("xT1sl", [128, NIC * 256], BF16))
            wkvf = p0.enter_context(nc.sbuf_tensor("wkvf", [128, NIC * 128], F32))
            wkvh = p0.enter_context(nc.sbuf_tensor("wkvh", [128, NIC * 256], BF16))
            wkvl = p0.enter_context(nc.sbuf_tensor("wkvl", [128, NIC * 256], BF16))
            kfm = p0.enter_context(nc.sbuf_tensor("kfm", [128, 256], F32))
            kraw = p0.enter_context(nc.sbuf_tensor("kraw", [128, 256], F32))
            krawh = p0.enter_context(nc.sbuf_tensor("krawh", [128, 256], BF16))
            krawl = p0.enter_context(nc.sbuf_tensor("krawl", [128, 256], BF16))
            ktmp = p0.enter_context(nc.sbuf_tensor("ktmp", [128, 256], F32))

            wkv_marks = []
            for wh in range(2):
                def _wkv_ld(sync, wh=wh, pm=wkv_marks[-1][1] if wkv_marks else 0):
                    if wh == 0:
                        wge(sync, "pre", pre_sd)
                        wge(sync, "v", kap_v)
                        wge(sync, "s", kap_s)
                    else:
                        wge(sync, "v", pm)
                    for ic in range(NIC):
                        inc(sync.dma_start(out=wkvf[:, ic * 128:(ic + 1) * 128],
                                           in_=wkv[ic, :, wh * 128:(wh + 1) * 128]), "wkv", 16)
                emit("sync", _wkv_ld)
                wsd = cnt["wkv"]
                hi3 = wkvh[:].rearrange("p (ic c) -> p ic c", c=256)[:, :, wh * 128:(wh + 1) * 128]
                lo3 = wkvl[:].rearrange("p (ic c) -> p ic c", c=256)[:, :, wh * 128:(wh + 1) * 128]
                src3 = wkvf[:].rearrange("p (ic c) -> p ic c", c=128)
                smk, vmk = split_hl(src3, hi3, lo3, waits_s=(("wkv", wsd),), waits_v=(("wkv", wsd),))
                wkv_marks.append((smk, vmk))
            wkv_s, wkv_v = wkv_marks[-1]

            xsub_free_t = 0
            xT1_free_t = 0
            cs_free_v = 0

            for hsi in range(SH_R // 256):
                r0 = hsi * 256

                def _ld(g, r0=r0, xft=xsub_free_t):
                    wge(g, "t", xft)
                    if r0 == 0:
                        wge(g, "v", wkv_v)
                    for tc in range(2):
                        inc(g.dma_start(out=xsub[:, tc * D:(tc + 1) * D],
                                        in_=xs[r0 + tc * 128:r0 + (tc + 1) * 128, :]), "gd", 16)
                    gflush(g)
                emit("gpsimd", _ld)
                ld_gd = cnt["g"]

                def _cs(sync, r0=r0, csf=cs_free_v):
                    wge(sync, "v", csf)
                    inc(sync.dma_start(out=costt[:, 0:256], in_=cosT[:, r0:r0 + 256]), "cs", 16)
                    inc(sync.dma_start(out=sintt[:, 0:256], in_=sinT[:, r0:r0 + 256]), "cs", 16)
                emit("sync", _cs)
                cs_sd = cnt["cs"]

                for tc in range(2):
                    ln_chunk(xsub[:, tc * D:(tc + 1) * D], xsub[:, tc * D:(tc + 1) * D],
                             base=tc * 8, waits=(("g", ld_gd),))
                ln_v = cnt["v"]

                srcs, dsth, dstl = [], [], []
                for tc in range(2):
                    for ic in range(NIC):
                        srcs.append(xsub[:, tc * D + ic * 128: tc * D + (ic + 1) * 128])
                        dsth.append(xT1sh[:, ic * 256 + tc * 128: ic * 256 + (tc + 1) * 128])
                        dstl.append(xT1sl[:, ic * 256 + tc * 128: ic * 256 + (tc + 1) * 128])
                fw = [("v", ln_v), ("t", xT1_free_t)]
                xT1_sdone, xT1_vdone = transpose_split(srcs, dsth, dstl, first_waits=tuple(fw))
                xsub_free_t = cnt["t"]

                if 256 <= r0 < SH_R - 256:
                    t_idx = (r0 - 256) // 512
                    half = ((r0 - 256) % 512) // 256

                    def _spill(g, t_idx=t_idx, half=half, mk=xT1_sdone, mkv=xT1_vdone):
                        wge(g, "s", mk)
                        wge(g, "v", mkv)
                        for ic in range(NIC):
                            inc(g.dma_start(out=xT1h_sc[t_idx, ic, :, half * 256:half * 256 + 256],
                                            in_=xT1sh[:, ic * 256:(ic + 1) * 256]), "gd", 16)
                            inc(g.dma_start(out=xT1l_sc[t_idx, ic, :, half * 256:half * 256 + 256],
                                            in_=xT1sl[:, ic * 256:(ic + 1) * 256]), "gd", 16)
                    emit("gpsimd", _spill)

                def _kmm(tensor, mk=xT1_sdone, mkv=xT1_vdone):
                    wge(tensor, "s", preb_s)
                    wge(tensor, "s", mk)
                    wge(tensor, "v", mkv)
                    wge(tensor, "v", wkv_v)
                    wge(tensor, "s", wkv_s)
                    for ic in range(NIC):
                        mm3(tensor, psC[:, 0:256],
                            wkvh[:, ic * 256:ic * 256 + 128], wkvl[:, ic * 256:ic * 256 + 128],
                            xT1sh[:, ic * 256:(ic + 1) * 256], xT1sl[:, ic * 256:(ic + 1) * 256],
                            ic == 0, ic == NIC - 1)
                    inc(tensor.last, "t")
                emit("tensor", _kmm)
                kmm_t = cnt["t"]

                smk, vmk = split_hl(psC[:, 0:256], krawh[:], krawl[:],
                                    waits_s=(("t", kmm_t),), waits_v=(("t", kmm_t),))
                kraw_s = smk
                kraw_v = vmk

                def _kc(vector, kmm_t=kmm_t, cs_sd=cs_sd):
                    wge(vector, "t", kmm_t)
                    wge(vector, "cs", cs_sd)
                    vector.tensor_tensor(kfm[:], psC[:, 0:256], costt[:, 0:256], ALU.mult)
                    vector.drain()
                    inc(vector.last, "v")
                emit("vector", _kc)
                kc_v = cnt["v"]

                def _krot(tensor, kraw_s=kraw_s, kraw_v=kraw_v, kc_v=kc_v):
                    wge(tensor, "s", kraw_s)
                    wge(tensor, "v", kraw_v)
                    wge(tensor, "v", kc_v)
                    tensor.matmul(psC[:, 0:256], protB[:], krawh[:], start=True, stop=False)
                    tensor.matmul(psC[:, 0:256], protB[:], krawl[:], start=False, stop=True)
                    inc(tensor.last, "t")
                emit("tensor", _krot)
                krot_t = cnt["t"]

                def _kr2(vector, krot_t=krot_t, r0=r0):
                    wge(vector, "t", krot_t)
                    vector.tensor_tensor(ktmp[:], psC[:, 0:256], sintt[:, 0:256], ALU.mult)
                    vector.drain()
                    vector.tensor_tensor(kraw[:], kfm[:], ktmp[:], ALU.add)
                    vector.drain()
                    inc(vector.last, "v")
                emit("vector", _kr2)
                kr_f_v = cnt["v"]
                split_hl(kraw[:], krTh[:, r0:r0 + 256], krTl[:, r0:r0 + 256],
                         waits_s=(("v", kr_f_v),))
                cs_free_v = cnt["v"]
                krot_done_v = cnt["v"]

                for tc in range(2):
                    def _vmm(tensor, tc=tc, kv=krot_done_v, pv=cnt["v"], ps=cnt["s"]):
                        if tc == 0:
                            wge(tensor, "v", kv)
                            wge(tensor, "s", ps)
                        else:
                            wge(tensor, "v", pv)
                        for ic in range(NIC):
                            mm3(tensor, psD[:, 0:128],
                                xT1sh[:, ic * 256 + tc * 128: ic * 256 + (tc + 1) * 128],
                                xT1sl[:, ic * 256 + tc * 128: ic * 256 + (tc + 1) * 128],
                                wkvh[:, ic * 256 + 128:(ic + 1) * 256],
                                wkvl[:, ic * 256 + 128:(ic + 1) * 256],
                                ic == 0, ic == NIC - 1)
                        inc(tensor.last, "t")
                    emit("tensor", _vmm)
                    vmm_t = cnt["t"]

                    kc = r0 // 128 + tc
                    smk, vmk = split_hl(psD[:, 0:128],
                                        vaugh[:, kc * 129: kc * 129 + 128],
                                        vaugl[:, kc * 129: kc * 129 + 128],
                                        waits_s=(("t", vmm_t),), waits_v=(("t", vmm_t),))

                    def _vone(vector, kc=kc):
                        vector.memset(vaugh[:, kc * 129 + 128: kc * 129 + 129], 1.0)
                        vector.drain()
                        vector.memset(vaugl[:, kc * 129 + 128: kc * 129 + 129], 0.0)
                        vector.drain()
                        inc(vector.last, "v")
                    emit("vector", _vone)
                xT1_free_t = cnt["t"]

            kv_ready_v = cnt["v"]
            kv_ready_s = cnt["s"]
            p0_cs_free = cs_free_v
        p0_gd = cnt["gd"]
        w_ready = {}
        stw = contextlib.ExitStack()
        with stw:
            wchk = stw.enter_context(nc.sbuf_tensor("wchk", [128, 2 * 2048], F32, side="right"))
            wchkh = stw.enter_context(nc.sbuf_tensor("wchkh", [128, 4 * 2048], BF16, side="right"))
            wchkl = stw.enter_context(nc.sbuf_tensor("wchkl", [128, 4 * 2048], BF16, side="right"))
            wn = 0
            spl_marks = []
            sp_rounds = [0, 0, 0, 0]
            wjobs = [
                ("wq", wq, wqh_sc, wql_sc, 16, 128, H * NIC // 16),
                ("wo", wo, woh_sc, wol_sc, 4, 512, 8 * NIC // 4),
                ("wb", worg, wbh_sc, wbl_sc, 4, 512, 8 * NIC // 4),
                ("ld", ldn, ldh_sc, ldl_sc, 16, 128, 4 * NIC // 16),
                ("lu", lup, luh_sc, lul_sc, 4, 512, 8 * 4 // 4),
            ]
            for jname, srcap, dsth_ap, dstl_ap, ntile, tcols, nchunks in wjobs:
                src_flat = srcap.rearrange("a b p c -> (a b) p c")
                dh_flat = dsth_ap.rearrange("a b p c -> (a b) p c")
                dl_flat = dstl_ap.rearrange("a b p c -> (a b) p c")
                for ci in range(nchunks):
                    par = "wla" if wn % 2 == 0 else "wlb"
                    spn = "sp%d" % (wn % 4)

                    def _wld(sync, jn=jname, ci=ci, wn=wn, par=par, ntile=ntile, tcols=tcols,
                             src_flat=src_flat, ft=kap_t_f, fv=kap_v, fs=kap_s):
                        if wn >= 2:
                            sm, vm = spl_marks[wn - 2]
                            wge(sync, "s", sm)
                            wge(sync, "v", vm)
                        else:
                            wge(sync, "t", ft)
                            wge(sync, "v", fv)
                            wge(sync, "s", fs)
                        inc(sync.dma_start(out=wchk[:, (wn % 2) * 2048:(wn % 2) * 2048 + 2048]
                                           .rearrange("p (a c) -> p a c", c=tcols),
                                           in_=src_flat[ci * ntile:(ci + 1) * ntile].transpose([1, 0, 2])),
                            par, 16)
                    emit("sync", _wld)
                    wld_mark = cnt[par]

                    hi_sl = wchkh[:, (wn % 4) * 2048:(wn % 4) * 2048 + 2048]
                    lo_sl = wchkl[:, (wn % 4) * 2048:(wn % 4) * 2048 + 2048]
                    ws = ((par, wld_mark), (spn, sp_rounds[wn % 4] * 32))
                    smk, vmk = split_hl(wchk[:, (wn % 2) * 2048:(wn % 2) * 2048 + 2048],
                                        hi_sl, lo_sl, waits_s=ws, waits_v=ws)
                    spl_marks.append((smk, vmk))

                    def _wsp(g, jn=jname, ci=ci, wn=wn, spn=spn, smk=smk, vmk=vmk,
                             hi_sl=hi_sl, lo_sl=lo_sl, ntile=ntile, tcols=tcols,
                             dh_flat=dh_flat, dl_flat=dl_flat):
                        wge(g, "s", smk)
                        wge(g, "v", vmk)
                        inc(g.dma_start(out=dh_flat[ci * ntile:(ci + 1) * ntile].transpose([1, 0, 2]),
                                        in_=hi_sl.rearrange("p (a c) -> p a c", c=tcols)), spn, 16)
                        inc(g.dma_start(out=dl_flat[ci * ntile:(ci + 1) * ntile].transpose([1, 0, 2]),
                                        in_=lo_sl.rearrange("p (a c) -> p a c", c=tcols)), spn, 16)
                    emit("gpsimd", _wsp)
                    sp_rounds[wn % 4] += 1
                    wn += 1

                def _wclose(g):
                    for i in range(4):
                        wge(g, "sp%d" % i, cnt["sp%d" % i])
                    inc(g.engine_nop(), "g")
                emit("gpsimd", _wclose)
                w_ready[jname] = cnt["g"]
            w_ready_all = cnt["g"]
            for k in list(w_ready):
                w_ready[k] = max(w_ready[k], 0)


        if STAGE < 3:
            def _fini0(g):
                g.wait_ge(sems["gd"], cnt["gd"])
                g.wait_ge(sems["v"], cnt["v"])
                g.wait_ge(sems["s"], cnt["s"])
                g.wait_ge(sems["t"], cnt["t"])
            emit("gpsimd", _fini0)
            return nc
        res1_gd = [0] * 16
        x2T_gd = [0] * NT
        csq_free = p0_cs_free
        xT1t_free_prev = 0
        y_free_prev_v = 0
        x2tp_prev_gd = 0
        route_red_v = 0

        for t in range(NT if STAGE >= 4 else 1):
            r0q = 256 + t * TQ
            banks_clear_s = cnt["s"]
            banks_clear_v = cnt["v"]

            st_q = contextlib.ExitStack()
            qrTh = st_q.enter_context(nc.sbuf_tensor(f"qrTh{t}", [128, H * 512], BF16))
            qrTl = st_q.enter_context(nc.sbuf_tensor(f"qrTl{t}", [128, H * 512], BF16))

            st_a2 = contextlib.ExitStack()
            xT1th = st_a2.enter_context(nc.sbuf_tensor(f"xT1th{t}", [128, NIC * 512], BF16))
            xT1tl = st_a2.enter_context(nc.sbuf_tensor(f"xT1tl{t}", [128, NIC * 512], BF16))
            wringqh = st_a2.enter_context(nc.sbuf_tensor(f"wringqh{t}", [128, 2 * 1024], BF16))
            wringql = st_a2.enter_context(nc.sbuf_tensor(f"wringql{t}", [128, 2 * 1024], BF16))
            qrah = st_a2.enter_context(nc.sbuf_tensor(f"qrah{t}", [128, 2 * 512], BF16))
            qral = st_a2.enter_context(nc.sbuf_tensor(f"qral{t}", [128, 2 * 512], BF16))
            qca = st_a2.enter_context(nc.sbuf_tensor(f"qca{t}", [128, 2 * 512], F32))
            qsa = st_a2.enter_context(nc.sbuf_tensor(f"qsa{t}", [128, 2 * 512], F32))

            def _csq(sync, r0q=r0q, csf=csq_free):
                wge(sync, "v", csf)
                inc(sync.dma_start(out=costt[:], in_=cosT[:, r0q:r0q + 512]), "cs", 16)
                inc(sync.dma_start(out=sintt[:], in_=sinT[:, r0q:r0q + 512]), "cs", 16)
            emit("sync", _csq)
            csq_sd = cnt["cs"]

            xq_marks = [0, 0, 0, 0]

            def _ldx(g, t=t, xf=xT1t_free_prev, fv=cnt["v"], fs=cnt["s"], pg=cnt["gd"],
                     sp_now=tuple(cnt["sp%d" % i] for i in range(4))):
                wge(g, "t", xf)
                wge(g, "v", fv)
                wge(g, "s", fs)
                wge(g, "gd", pg)
                for i in range(4):
                    wge(g, "sp%d" % i, sp_now[i])
                for q in range(4):
                    spn = "sp%d" % q
                    for ic in range(q * 8, (q + 1) * 8):
                        inc(g.dma_start(out=xT1th[:, ic * 512:(ic + 1) * 512], in_=xT1h_sc[t, ic]), spn, 16)
                        inc(g.dma_start(out=xT1tl[:, ic * 512:(ic + 1) * 512], in_=xT1l_sc[t, ic]), spn, 16)
                    xq_marks[q] = cnt[spn]
            emit("gpsimd", _ldx)

            q_use = []
            rope_raw_s = []
            rope_rot_t = []
            rope_done_v = []
            wq_split_s = []
            wq_split_v = []

            def emit_rope(hd):
                qsl = slice((hd % 2) * 512, (hd % 2) * 512 + 512)
                ws = (("t", q_use[hd * 4 + 3]),) + ((("t", rope_rot_t[hd - 2]),) if hd >= 2 else ())
                smk, vmk = split_hl(bank(hd % 2), qrah[:, qsl], qral[:, qsl],
                                    waits_s=ws, waits_v=ws)
                rope_raw_s.append(cnt["s"])
                rope_raw_vmk = vmk

                def _rv(vector, hd=hd, csq_sd=csq_sd):
                    wge(vector, "t", q_use[hd * 4 + 3])
                    if hd == 0:
                        wge(vector, "cs", csq_sd)
                    vector.tensor_tensor(qca[:, (hd % 2) * 512:(hd % 2) * 512 + 512],
                                         bank(hd % 2), costt[:], ALU.mult)
                    vector.drain()
                    inc(vector.last, "v")
                emit("vector", _rv)
                qc_v = cnt["v"]

                def _rt(tensor, hd=hd, qc_v=qc_v, rvm=rope_raw_vmk, qsl=qsl):
                    wge(tensor, "s", rope_raw_s[hd])
                    wge(tensor, "v", rvm)
                    wge(tensor, "v", qc_v)
                    if hd >= 2:
                        wge(tensor, "v", rope_done_v[hd - 2])
                    tensor.matmul(bank(2 + hd % 2), protB[:], qrah[:, qsl], start=True, stop=False)
                    tensor.matmul(bank(2 + hd % 2), protB[:], qral[:, qsl], start=False, stop=True)
                    inc(tensor.last, "t")
                emit("tensor", _rt)
                rope_rot_t.append(cnt["t"])

                def _rv2(vector, hd=hd):
                    wge(vector, "t", rope_rot_t[hd])
                    vector.tensor_tensor(qsa[:, (hd % 2) * 512:(hd % 2) * 512 + 512],
                                         bank(2 + hd % 2), sintt[:], ALU.mult)
                    vector.drain()
                    vector.tensor_tensor(qsa[:, (hd % 2) * 512:(hd % 2) * 512 + 512],
                                         qca[:, (hd % 2) * 512:(hd % 2) * 512 + 512],
                                         qsa[:, (hd % 2) * 512:(hd % 2) * 512 + 512], ALU.add)
                    vector.drain()
                    inc(vector.last, "v")
                emit("vector", _rv2)
                qr_v = cnt["v"]
                split_hl(qsa[:, (hd % 2) * 512:(hd % 2) * 512 + 512],
                         qrTh[:, hd * 512:(hd + 1) * 512],
                         qrTl[:, hd * 512:(hd + 1) * 512],
                         waits_s=(("v", qr_v),))
                rope_done_v.append(cnt["v"])


            for hd in range(H):
                for half in range(4):
                    n = hd * 4 + half

                    def _wql(sync, hd=hd, half=half, n=n, ft=cnt["t"], fv=cnt["v"], fs=cnt["s"],
                             wr=w_ready_all):
                        if n >= 2:
                            wge(sync, "t", q_use[n - 2])
                        else:
                            wge(sync, "t", ft)
                            wge(sync, "v", fv)
                            wge(sync, "s", fs)
                            wge(sync, "g", wr)
                        par = "wqa" if n % 2 == 0 else "wqb"
                        inc(sync.dma_start(out=wringqh[:, (n % 2) * 1024:(n % 2) * 1024 + 1024]
                                           .rearrange("p (a c) -> p a c", c=128),
                                           in_=wqh_sc[hd, half * 8:(half + 1) * 8].transpose([1, 0, 2])), par, 16)
                        inc(sync.dma_start(out=wringql[:, (n % 2) * 1024:(n % 2) * 1024 + 1024]
                                           .rearrange("p (a c) -> p a c", c=128),
                                           in_=wql_sc[hd, half * 8:(half + 1) * 8].transpose([1, 0, 2])), par, 16)
                    emit("sync", _wql)
                    wsd = cnt["wqa" if n % 2 == 0 else "wqb"]
                    wq_split_s.append(wsd)

                    sl_h = wringqh[:, (n % 2) * 1024:(n % 2) * 1024 + 1024]
                    sl_l = wringql[:, (n % 2) * 1024:(n % 2) * 1024 + 1024]

                    def _qmm(tensor, hd=hd, half=half, n=n, bc=banks_clear_s, bv=banks_clear_v,
                             sl_h=sl_h, sl_l=sl_l):
                        wge(tensor, "wqa" if n % 2 == 0 else "wqb", wq_split_s[n])
                        wge(tensor, "sp%d" % half, xq_marks[half])
                        if half == 0:
                            if hd == 0:
                                wge(tensor, "s", bc)
                                wge(tensor, "v", bv)
                            if hd >= 2:
                                wge(tensor, "v", rope_done_v[hd - 2])
                                wge(tensor, "s", rope_raw_s[hd - 2])
                        qb = bank(hd % 2)
                        for i2 in range(8):
                            ic = half * 8 + i2
                            mm3(tensor, qb,
                                sl_h[:, i2 * 128:(i2 + 1) * 128], sl_l[:, i2 * 128:(i2 + 1) * 128],
                                xT1th[:, ic * 512:(ic + 1) * 512], xT1tl[:, ic * 512:(ic + 1) * 512],
                                ic == 0, ic == NIC - 1)
                        inc(tensor.last, "t")
                    emit("tensor", _qmm)
                    q_use.append(cnt["t"])

                if hd >= 1:
                    emit_rope(hd - 1)

            emit_rope(H - 1)
            xT1t_free_cur = q_use[-1]
            csq_free = cnt["v"]
            rope_all_v = cnt["v"]
            rope_all_s = cnt["s"]
            st_a2.close()

            st_a3 = contextlib.ExitStack()
            sigf = st_a3.enter_context(nc.sbuf_tensor(f"sigf{t}", [128, 1536], F32))
            sigh = st_a3.enter_context(nc.sbuf_tensor(f"sigh{t}", [128, 1536], BF16))
            sigl = st_a3.enter_context(nc.sbuf_tensor(f"sigl{t}", [128, 1536], BF16))
            uall = st_a3.enter_context(nc.sbuf_tensor(f"uall{t}", [128, 8 * 516], F32))
            atm = st_a3.enter_context(nc.sbuf_tensor(f"atm{t}", [128, 1024], F32))
            btmp = st_a3.enter_context(nc.sbuf_tensor(f"btmp{t}", [128, 128], F32))
            bsm = st_a3.enter_context(nc.sbuf_tensor(f"bsm{t}", [128, 32], F32))
            st_fm = contextlib.ExitStack()
            afmh = st_fm.enter_context(nc.sbuf_tensor(f"afmh{t}", [128, NIC * 512], BF16, side="right"))
            afml = st_fm.enter_context(nc.sbuf_tensor(f"afml{t}", [128, NIC * 512], BF16, side="right"))

            sig_cons_t = 0
            sc_evac_s = [banks_clear_s, banks_clear_s]
            uev_s = banks_clear_s
            tr_ev_s = banks_clear_s
            tr_ev_v = banks_clear_v
            pending_uev = None
            last_blend_v = 0
            last_tr_t = 0

            smm_marks = {}

            def emit_scores(qh2, h):
                qloc = 2 * t + qh2
                hg, hp = h // 8, h % 8
                pset = psA if h % 2 == 0 else psB
                def _smm2(tensor, h=h, qh2=qh2, qloc=qloc, pset=pset,
                         scv=sc_evac_s[h % 2], rav=rope_all_v, ras=rope_all_s, t=t):
                    if h < 2 and qh2 == 0:
                        wge(tensor, "v", rav)
                        wge(tensor, "s", ras)
                        if t == 0:
                            wge(tensor, "v", kv_ready_v)
                            wge(tensor, "s", kv_ready_s)
                    wge(tensor, "s", scv)
                    for i in range(6):
                        kc = 2 * qloc + i
                        has_mask = (t == 0 and qh2 == 0 and i < 2) or \
                                   (t == NT - 1 and qh2 == 1 and i >= 4)
                        q_h = qrTh[:, h * 512 + qh2 * 256: h * 512 + qh2 * 256 + 256]
                        q_l = qrTl[:, h * 512 + qh2 * 256: h * 512 + qh2 * 256 + 256]
                        mm3(tensor, pset[:, i * 256:(i + 1) * 256],
                            krTh[:, kc * 128:(kc + 1) * 128], krTl[:, kc * 128:(kc + 1) * 128],
                            q_h, q_l, True, not has_mask)
                        if has_mask:
                            mi = kc if kc < 2 else kc - 16
                            tensor.matmul(pset[:, i * 256:(i + 1) * 256],
                                      maskB[0:1, mi * 128:(mi + 1) * 128],
                                      onesB[0:1, 0:256], start=False, stop=True)
                    inc(tensor.last, "t")
                emit("tensor", _smm2)
                smm_marks[(qh2, h)] = cnt["t"]
            for qh2 in range(2):
                qloc = 2 * t + qh2
                for h in range(H):
                    hg, hp = h // 8, h % 8
                    pset = psA if h % 2 == 0 else psB

                    if (qh2, h) not in smm_marks:
                        emit_scores(qh2, h)
                    smm_t = smm_marks[(qh2, h)]

                    def _sgev(scalar, pset=pset, smm_t=smm_t, sct=sig_cons_t):
                        wge(scalar, "t", smm_t)
                        wge(scalar, "t", sct)
                        scalar.activation(sigf[:], pset[:, 0:1536], AF.Sigmoid, scale=SCALE)
                        scalar.drain()
                        inc(scalar.last, "s")
                    emit("scalar", _sgev)
                    sc_evac_s[h % 2] = cnt["s"]
                    sgev_s = cnt["s"]
                    if pending_uev is not None:
                        emit("scalar", pending_uev)
                        uev_s = cnt["s"]
                        pending_uev = None

                    def _slo(vector, sgev_s=sgev_s, sct=sig_cons_t):
                        wge(vector, "s", sgev_s)
                        wge(vector, "t", sct)
                        vector.tensor_scalar(sigh[:], sigf[:], 1.0, None, ALU.mult)
                        vector.drain()
                        vector.tensor_tensor(sigl[:], sigf[:], sigh[:], ALU.subtract)
                        vector.drain()
                        inc(vector.last, "v")
                    emit("vector", _slo)
                    slo_v = cnt["v"]
                    nxt = (qh2, h + 1) if h + 1 < H else ((1, 0) if qh2 == 0 else None)
                    if nxt is not None and nxt not in smm_marks:
                        emit_scores(*nxt)

                    def _umm(tensor, qloc=qloc, sgev_s=sgev_s, slo_v=slo_v, ue=uev_s,
                             tre=tr_ev_s, trev=tr_ev_v):
                        wge(tensor, "s", sgev_s)
                        wge(tensor, "v", slo_v)
                        wge(tensor, "s", ue)
                        wge(tensor, "s", tre)
                        wge(tensor, "v", trev)
                        slots = {(0, 0): psC[:, 0:129], (1, 0): psC[:, 129:258],
                                 (0, 1): psC[:, 258:387], (1, 1): psD[:, 0:129]}
                        for side in range(2):
                            kcs = [2 * qloc + 2 * side + j for j in range(4)]
                            for qc2 in range(2):
                                sl = slots[(side, qc2)]
                                for j, kc in enumerate(kcs):
                                    i = kc - 2 * qloc
                                    mm3(tensor, sl,
                                        sigh[:, i * 256 + qc2 * 128: i * 256 + qc2 * 128 + 128],
                                        sigl[:, i * 256 + qc2 * 128: i * 256 + qc2 * 128 + 128],
                                        vaugh[:, kc * 129:(kc + 1) * 129],
                                        vaugl[:, kc * 129:(kc + 1) * 129],
                                        j == 0, j == 3)
                        inc(tensor.last, "t")
                    emit("tensor", _umm)
                    sig_cons_t = cnt["t"]
                    umm_t = cnt["t"]

                    def _uev(scalar, hp=hp, umm_t=umm_t, lbv=last_blend_v):
                        wge(scalar, "t", umm_t)
                        if hp == 0:
                            wge(scalar, "v", lbv)
                        scalar.copy(uall[:, hp * 516: hp * 516 + 387], psC[:, 0:387])
                        scalar.drain()
                        scalar.copy(uall[:, hp * 516 + 387: hp * 516 + 516], psD[:, 0:129])
                        scalar.drain()
                        inc(scalar.last, "s")
                    if hp == 7:
                        emit("scalar", _uev)
                        uev_s = cnt["s"]
                    else:
                        pending_uev = _uev

                    if hp == 7:
                        for qc2 in range(2):
                            qcg = t * 4 + qh2 * 2 + qc2

                            def _bl(vector, qc2=qc2, qcg=qcg, ue=uev_s, ltt=last_tr_t):
                                wge(vector, "s", ue)
                                wge(vector, "t", ltt)
                                rl = bsm[:, 0:8]
                                rr = bsm[:, 8:16]
                                c1 = bsm[:, 16:24]
                                c2 = bsm[:, 24:32]
                                base_c = qc2 * 258
                                vector.tensor_copy(rl, uall[:, base_c + 128: base_c + 128 + 7 * 516 + 1: 516])
                                vector.drain()
                                vector.tensor_copy(rr, uall[:, base_c + 257: base_c + 257 + 7 * 516 + 1: 516])
                                vector.drain()
                                vector.reciprocal(rl, rl)
                                vector.drain()
                                vector.reciprocal(rr, rr)
                                vector.drain()
                                vector.tensor_scalar(c1, rl, alpmt[:, qcg:qcg + 1], None, ALU.mult)
                                vector.drain()
                                vector.tensor_scalar(c2, rr, alpt[:, qcg:qcg + 1], None, ALU.mult)
                                vector.drain()
                                for hp2 in range(8):
                                    if qc2 == 0:
                                        lsl = uall[:, hp2 * 516: hp2 * 516 + 128]
                                        rsl = uall[:, hp2 * 516 + 129: hp2 * 516 + 257]
                                    else:
                                        lsl = uall[:, hp2 * 516 + 258: hp2 * 516 + 386]
                                        rsl = uall[:, hp2 * 516 + 387: hp2 * 516 + 515]
                                    vector.tensor_scalar(btmp[:], rsl, c2[:, hp2:hp2 + 1], None, ALU.mult)
                                    vector.drain()
                                    vector.scalar_tensor_tensor(atm[:, hp2 * 128:(hp2 + 1) * 128],
                                                                lsl, c1[:, hp2:hp2 + 1], btmp[:],
                                                                ALU.mult, ALU.add)
                                    vector.drain()
                                inc(vector.last, "v")
                            emit("vector", _bl)
                            last_blend_v = cnt["v"]

                            def _tr(tensor, blm=last_blend_v, ue=uev_s, tre=tr_ev_s, trev=tr_ev_v):
                                wge(tensor, "v", blm)
                                wge(tensor, "s", ue)
                                wge(tensor, "s", tre)
                                wge(tensor, "v", trev)
                                for hp2 in range(8):
                                    dst = psC if hp2 < 4 else psD
                                    tensor.transpose(dst[:, (hp2 % 4) * 128:(hp2 % 4) * 128 + 128],
                                                     atm[:, hp2 * 128:(hp2 + 1) * 128], idf[:])
                                inc(tensor.last, "t")
                            emit("tensor", _tr)
                            last_tr_t = cnt["t"]

                            def _trevs(scalar, hg=hg, qh2=qh2, qc2=qc2, tr_t=last_tr_t):
                                wge(scalar, "t", tr_t)
                                qcol = (qh2 * 2 + qc2) * 128
                                a3h = afmh[:].rearrange("p (h q) -> p h q", q=512)
                                scalar.copy(a3h[:, hg * 8: hg * 8 + 4, qcol:qcol + 128],
                                            psC[:, 0:512].rearrange("p (a b) -> p a b", b=128))
                                scalar.drain()
                                scalar.copy(a3h[:, hg * 8 + 4: hg * 8 + 8, qcol:qcol + 128],
                                            psD[:, 0:512].rearrange("p (a b) -> p a b", b=128))
                                scalar.drain()
                                inc(scalar.last, "s")
                            emit("scalar", _trevs)
                            tr_ev_s = cnt["s"]

                            def _trevv(vector, hg=hg, qh2=qh2, qc2=qc2, tr_t=last_tr_t, ses=tr_ev_s):
                                wge(vector, "t", tr_t)
                                wge(vector, "s", ses)
                                qcol = (qh2 * 2 + qc2) * 128
                                a3h = afmh[:].rearrange("p (h q) -> p h q", q=512)
                                a3l = afml[:].rearrange("p (h q) -> p h q", q=512)
                                vector.tensor_tensor(a3l[:, hg * 8: hg * 8 + 4, qcol:qcol + 128],
                                                     psC[:, 0:512].rearrange("p (a b) -> p a b", b=128),
                                                     a3h[:, hg * 8: hg * 8 + 4, qcol:qcol + 128], ALU.subtract)
                                vector.drain()
                                vector.tensor_tensor(a3l[:, hg * 8 + 4: hg * 8 + 8, qcol:qcol + 128],
                                                     psD[:, 0:512].rearrange("p (a b) -> p a b", b=128),
                                                     a3h[:, hg * 8 + 4: hg * 8 + 8, qcol:qcol + 128], ALU.subtract)
                                vector.drain()
                                inc(vector.last, "v")
                            emit("vector", _trevv)
                            tr_ev_v = cnt["v"]
                            uev_s = tr_ev_s

            st_a3.close()
            st_q.close()

            st_a4 = contextlib.ExitStack()
            yb = st_a4.enter_context(nc.sbuf_tensor(f"yb{t}", [128, 2 * D], F32))
            wringoh = st_a4.enter_context(nc.sbuf_tensor(f"wringoh{t}", [128, 4 * 512], BF16))
            wringol = st_a4.enter_context(nc.sbuf_tensor(f"wringol{t}", [128, 4 * 512], BF16))
            xinp = st_a4.enter_context(nc.sbuf_tensor(f"xinp{t}", [128, D], F32))
            resp = st_a4.enter_context(nc.sbuf_tensor(f"resp{t}", [128, D], F32))
            x2p = xinp
            x2tph = st_a4.enter_context(nc.sbuf_tensor(f"x2tph{t}", [128, D], BF16))
            x2tpl = st_a4.enter_context(nc.sbuf_tensor(f"x2tpl{t}", [128, D], BF16))

            wo_use = []
            wo_split_s = []
            wo_split_v = []
            attn_clear_s = cnt["s"]
            attn_clear_v = cnt["v"]
            prev_x2t_s = 0
            prev_x2t_v = 0

            for halfq in range(2):
                y_ev_s = [0] * 8
                for og in range(8):
                    for ic in range(NIC):
                        ni = halfq * 8 * NIC + og * NIC + ic

                        def _wol(sync, og=og, ic=ic, ni=ni, ft=cnt["t"], fv=cnt["v"], fs=cnt["s"],
                                 wr=w_ready_all):
                            if ni >= 2:
                                wge(sync, "t", wo_use[ni - 2])
                            else:
                                wge(sync, "t", ft)
                                wge(sync, "v", fv)
                                wge(sync, "s", fs)
                                wge(sync, "g", wr)
                            par = "woa" if ni % 2 == 0 else "wob"
                            inc(sync.dma_start(out=wringoh[:, (ni % 2) * 512:(ni % 2) * 512 + 512],
                                               in_=woh_sc[og, ic]), par, 16)
                            inc(sync.dma_start(out=wringol[:, (ni % 2) * 512:(ni % 2) * 512 + 512],
                                               in_=wol_sc[og, ic]), par, 16)
                        emit("sync", _wol)
                        wsd = cnt["woa" if ni % 2 == 0 else "wob"]
                        wo_split_s.append(wsd)

                        sl_h = wringoh[:, (ni % 2) * 512:(ni % 2) * 512 + 512]
                        sl_l = wringol[:, (ni % 2) * 512:(ni % 2) * 512 + 512]

                        def _wom(tensor, halfq=halfq, og=og, ic=ic, ni=ni,
                                 acs=attn_clear_s, acv=attn_clear_v, yfv=y_free_prev_v,
                                 sl_h=sl_h, sl_l=sl_l):
                            wge(tensor, "woa" if ni % 2 == 0 else "wob", wo_split_s[ni])
                            if ni == 0:
                                wge(tensor, "s", acs)
                                wge(tensor, "v", acv)
                                wge(tensor, "v", yfv)
                            if ic == 0 and og >= 2:
                                wge(tensor, "s", y_ev_s[og - 2])
                            pb = [0, 1] if og % 2 == 0 else [2, 3]
                            for tc2 in range(2):
                                tc = halfq * 2 + tc2
                                mm3(tensor, bank(pb[tc2]),
                                    afmh[:, ic * 512 + tc * 128: ic * 512 + tc * 128 + 128],
                                    afml[:, ic * 512 + tc * 128: ic * 512 + tc * 128 + 128],
                                    sl_h, sl_l, ic == 0, ic == NIC - 1)
                            inc(tensor.last, "t")
                        emit("tensor", _wom)
                        wo_use.append(cnt["t"])

                    def _yev(scalar, og=og, mk=wo_use[-1]):
                        wge(scalar, "t", mk)
                        pb = [0, 1] if og % 2 == 0 else [2, 3]
                        for tc2 in range(2):
                            scalar.copy(yb[:, tc2 * D + og * 512: tc2 * D + og * 512 + 512], bank(pb[tc2]))
                            scalar.drain()
                        inc(scalar.last, "s")
                    emit("scalar", _yev)
                    y_ev_s[og] = cnt["s"]

                wo_done_s = cnt["s"]

                for tc2 in range(2):
                    tc = halfq * 2 + tc2
                    ch = t * 4 + tc

                    def _ldxin(g, t=t, tc=tc, pv=cnt["v"], fs=cnt["s"], px=prev_x2t_s):
                        wge(g, "v", pv)
                        wge(g, "s", fs if px == 0 else px)
                        inc(g.dma_start(out=xinp[:],
                                        in_=xs[256 + t * 512 + tc * 128: 256 + t * 512 + (tc + 1) * 128, :]), "gd", 16)
                        gflush(g)
                    emit("gpsimd", _ldxin)
                    xin_gd = cnt["g"]

                    lnw = [("s", wo_done_s)]
                    if ch > 0:
                        lnw.append(("g", res1_gd[ch - 1]))
                    ln_chunk(yb[:, tc2 * D:(tc2 + 1) * D], resp[:], base=8, waits=tuple(lnw))

                    def _addx(vector, xin_gd=xin_gd):
                        wge(vector, "g", xin_gd)
                        vector.tensor_tensor(resp[:], resp[:], xinp[:], ALU.add)
                        vector.drain()
                        inc(vector.last, "v")
                    emit("vector", _addx)
                    res_v = cnt["v"]

                    def _spillr(g, ch=ch, res_v=res_v):
                        wge(g, "v", res_v)
                        inc(g.dma_start(out=res1_sc[ch], in_=resp[:]), "gd", 16)
                        gflush(g)
                    emit("gpsimd", _spillr)
                    res1_gd[ch] = cnt["g"]

                    ln2m = ln_chunk(resp[:], x2p[:], base=16, waits=(("v", res_v),))

                    srcs = [x2p[:, ic * 128:(ic + 1) * 128] for ic in range(NIC)]
                    dsth = [x2tph[:, ic * 128:(ic + 1) * 128] for ic in range(NIC)]
                    dstl = [x2tpl[:, ic * 128:(ic + 1) * 128] for ic in range(NIC)]
                    fw = [("v", ln2m)]
                    if tc == 0 and t > 0:
                        fw.append(("g", x2tp_prev_gd))
                    x2t_s, x2t_v = transpose_split(srcs, dsth, dstl, first_waits=tuple(fw), psum_off=4)
                    prev_x2t_s = x2t_s
                    prev_x2t_v = x2t_v

                    def _spillx2(g, t=t, tc=tc, x2t_s=x2t_s, x2t_v=x2t_v):
                        wge(g, "s", x2t_s)
                        wge(g, "v", x2t_v)
                        for ic in range(NIC):
                            inc(g.dma_start(out=x2Th_sc[t, ic, :, tc * 128:(tc + 1) * 128],
                                            in_=x2tph[:, ic * 128:(ic + 1) * 128]), "gd", 16)
                            inc(g.dma_start(out=x2Tl_sc[t, ic, :, tc * 128:(tc + 1) * 128],
                                            in_=x2tpl[:, ic * 128:(ic + 1) * 128]), "gd", 16)
                        gflush(g)
                    emit("gpsimd", _spillx2)
                    x2tp_prev_gd = cnt["g"]
                    if tc == 3:
                        x2T_gd[t] = cnt["g"]

                    def _rred(vector, t=t, tc=tc, x2t_s=x2t_s):
                        wge(vector, "s", x2t_s)
                        for ic in range(NIC):
                            vector.reduce_sum(smalls[:, 68:69], x2tph[:, ic * 128:(ic + 1) * 128], axis=AX.X)
                            vector.drain()
                            vector.reduce_sum(smalls[:, 69:70], x2tpl[:, ic * 128:(ic + 1) * 128], axis=AX.X)
                            vector.drain()
                            vector.tensor_tensor(zbuf[:, ic * 16 + t * 4 + tc: ic * 16 + t * 4 + tc + 1],
                                                 smalls[:, 68:69], smalls[:, 69:70], ALU.add)
                            vector.drain()
                        inc(vector.last, "v")
                    emit("vector", _rred)
                    x2tp_free_v = cnt["v"]

                    def _guard(scalar, vmk=x2tp_free_v):
                        wge(scalar, "v", vmk)
                        inc(scalar.copy(smalls[:, 67:68], cct[:, 2:3]), "s")
                    emit("scalar", _guard)

            route_red_v = cnt["v"]
            y_free_prev_v = cnt["v"]
            xT1t_free_prev = xT1t_free_cur
            st_a4.close()
            st_fm.close()

        if STAGE < 5:
            def _fini0(g):
                g.wait_ge(sems["gd"], cnt["gd"])
                g.wait_ge(sems["v"], cnt["v"])
                g.wait_ge(sems["s"], cnt["s"])
                g.wait_ge(sems["t"], cnt["t"])
            emit("gpsimd", _fini0)
            return nc
        router_sclear = cnt["s"]

        def _z32(vector):
            wge(vector, "v", route_red_v)
            vector.reduce_sum(z32[:], zbuf[:].rearrange("p (ic k) -> p ic k", k=16), axis=AX.X)
            vector.drain()
            inc(vector.last, "v")
        emit("vector", _z32)
        z32_v = cnt["v"]
        split_hl(z32[:], z32h[:], z32l[:], waits_s=(("v", z32_v),))
        z32s_v = cnt["v"]
        z32s_s = cnt["s"]

        def _rmm(tensor):
            wge(tensor, "v", z32s_v)
            wge(tensor, "s", z32s_s)
            wge(tensor, "s", router_sclear)
            for ic in range(NIC):
                mm3(tensor, psC[0:1, 0:NE],
                    z32h[:, ic:ic + 1], z32l[:, ic:ic + 1],
                    kaph[:, ic * NE:(ic + 1) * NE], kapl[:, ic * NE:(ic + 1) * NE],
                    ic == 0, ic == NIC - 1)
            inc(tensor.last, "t")
        emit("tensor", _rmm)
        rmm_t = cnt["t"]

        def _rev(scalar):
            wge(scalar, "t", rmm_t)
            scalar.copy(routs[:], psC[0:1, 0:NE])
            scalar.drain()
            inc(scalar.last, "s")
        emit("scalar", _rev)
        rev_s = cnt["s"]
        smk, vmk = split_hl(routs[:], routsh[:], routsl[:], waits_s=(("s", rev_s),))
        revh_s, revh_v = smk, vmk

        def _rbm(tensor):
            wge(tensor, "s", revh_s)
            wge(tensor, "v", revh_v)
            tensor.matmul(psD[0:NB, 0:NE], bmtB[:], routsh[:], start=True, stop=False)
            tensor.matmul(psD[0:NB, 0:NE], bmtB[:], routsl[:], start=False, stop=True)
            inc(tensor.last, "t")
        emit("tensor", _rbm)
        rbm_t = cnt["t"]

        def _rev2(scalar):
            wge(scalar, "t", rbm_t)
            scalar.copy(routp[:], psD[0:NB, 0:NE])
            scalar.drain()
            inc(scalar.last, "s")
        emit("scalar", _rev2)
        rev2_s = cnt["s"]

        def _cc(g):
            wge(g, "s", rev2_s)
            inc(g.dma_start(out=cc_in[:], in_=routp[:]), "gd", 16)
            g.wait_ge(sems["gd"], cnt["gd"])
            inc(g.collective_compute("AllReduce", ALU.add,
                                     replica_groups=[list(range(NCORE))],
                                     ins=[cc_in[:].opt()], outs=[cc_out[:].opt()]), "cc", 1)
            g.wait_ge(sems["cc"], cnt["cc"])
            inc(g.dma_start(out=rout2[:], in_=cc_out[:]), "gd", 16)
            gflush(g)
        emit("gpsimd", _cc)
        cc_gd = cnt["g"]

        smk, vmk = split_hl(rout2[:], rout2h[:], rout2l[:],
                            waits_s=(("g", cc_gd),), waits_v=(("g", cc_gd),))
        r2h_s, r2h_v = smk, vmk

        def _rsel(tensor):
            wge(tensor, "s", r2h_s)
            wge(tensor, "v", r2h_v)
            tensor.matmul(psC[0:1, 0:NE], bmpB[:], rout2h[:], start=True, stop=False)
            tensor.matmul(psC[0:1, 0:NE], bmpB[:], rout2l[:], start=False, stop=True)
            inc(tensor.last, "t")
        emit("tensor", _rsel)
        rsel_t = cnt["t"]

        def _rsm(scalar):
            wge(scalar, "t", rsel_t)
            scalar.copy(routs[:], psC[0:1, 0:NE])
            scalar.drain()
            inc(scalar.last, "s")
        emit("scalar", _rsm)
        rsm_s = cnt["s"]

        def _smx(vector):
            wge(vector, "s", rsm_s)
            vector.reduce_max(smalls[0:1, 70:71], routs[:], axis=AX.X)
            vector.drain()
            vector.tensor_scalar(routs[:], routs[:], smalls[0:1, 70:71], None, ALU.subtract)
            vector.drain()
            inc(vector.last, "v")
        emit("vector", _smx)
        smx_v = cnt["v"]

        def _sexp(scalar):
            wge(scalar, "v", smx_v)
            scalar.activation(routs[:], routs[:], AF.Exp, scale=ROUTE_C)
            scalar.drain()
            inc(scalar.last, "s")
        emit("scalar", _sexp)
        sexp_s = cnt["s"]

        def _snrm(vector):
            wge(vector, "s", sexp_s)
            vector.reduce_sum(smalls[0:1, 70:71], routs[:], axis=AX.X)
            vector.drain()
            vector.reciprocal(smalls[0:1, 71:72], smalls[0:1, 70:71])
            vector.drain()
            vector.tensor_scalar(coefs[:], routs[:], smalls[0:1, 71:72], None, ALU.mult)
            vector.drain()
            inc(vector.last, "v")
        emit("vector", _snrm)
        coef_v = cnt["v"]

        smk, vmk = split_hl(coefs[:], coefsh[:], coefsl[:], waits_s=(("v", coef_v),))
        cfh_s, cfh_v = smk, vmk

        def _cbc(tensor):
            wge(tensor, "s", cfh_s)
            wge(tensor, "v", cfh_v)
            tensor.matmul(psC[:, 0:NE], onesB[0:1, 0:128], coefsh[:], start=True, stop=False)
            tensor.matmul(psC[:, 0:NE], onesB[0:1, 0:128], coefsl[:], start=False, stop=True)
            inc(tensor.last, "t")
        emit("tensor", _cbc)
        cbc_t = cnt["t"]

        def _cbev(scalar):
            wge(scalar, "t", cbc_t)
            scalar.copy(bc8[:], psC[:, 0:NE])
            scalar.drain()
            inc(scalar.last, "s")
        emit("scalar", _cbev)
        cbev_s = cnt["s"]

        def _cerf(vector):
            wge(vector, "s", cbev_s)
            vector.tensor_copy(cer[0:64, :], bc8[0:64, 0:8:2])
            vector.drain()
            vector.tensor_copy(cer[64:128, :], bc8[64:128, 1:8:2])
            vector.drain()
            inc(vector.last, "v")
        emit("vector", _cerf)
        cer_v = cnt["v"]

        if STAGE < 6:
            def _fini0(g):
                g.wait_ge(sems["gd"], cnt["gd"])
                g.wait_ge(sems["v"], cnt["v"])
                g.wait_ge(sems["s"], cnt["s"])
                g.wait_ge(sems["t"], cnt["t"])
            emit("gpsimd", _fini0)
            return nc
        x2tt_free = 0
        b_clear_s = cbev_s
        prev_out_gd = 0
        prev_stg_gd = 0
        for t in range(NT):
            st_b = contextlib.ExitStack()
            x2tth = st_b.enter_context(nc.sbuf_tensor(f"x2tth{t}", [128, NIC * 512], BF16))
            x2ttl = st_b.enter_context(nc.sbuf_tensor(f"x2ttl{t}", [128, NIC * 512], BF16))
            wringbh = st_b.enter_context(nc.sbuf_tensor(f"wringbh{t}", [128, 4 * 512], BF16))
            wringbl = st_b.enter_context(nc.sbuf_tensor(f"wringbl{t}", [128, 4 * 512], BF16))
            ldnrh = st_b.enter_context(nc.sbuf_tensor(f"ldnrh{t}", [128, 2 * 2048], BF16))
            ldnrl = st_b.enter_context(nc.sbuf_tensor(f"ldnrl{t}", [128, 2 * 2048], BF16))
            hsf = st_b.enter_context(nc.sbuf_tensor(f"hsf{t}", [128, 512], F32))
            hsh = st_b.enter_context(nc.sbuf_tensor(f"hsh{t}", [128, 4 * 512], BF16))
            hsl = st_b.enter_context(nc.sbuf_tensor(f"hsl{t}", [128, 4 * 512], BF16))
            stg = st_b.enter_context(nc.sbuf_tensor(f"stg{t}", [128, 4 * 512], F32))
            luprh = st_b.enter_context(nc.sbuf_tensor(f"luprh{t}", [128, 2 * 512], BF16))
            luprl = st_b.enter_context(nc.sbuf_tensor(f"luprl{t}", [128, 2 * 512], BF16))
            ffb = st_b.enter_context(nc.sbuf_tensor(f"ffb{t}", [128, D], F32, side="right"))
            rrb = st_b.enter_context(nc.sbuf_tensor(f"rrb{t}", [128, D], F32, side="right"))

            x2q_marks = [0, 0, 0, 0]

            def _ldx2(g, t=t, xf=x2tt_free, fv=cnt["v"], fs=cnt["s"], pg=cnt["gd"],
                      sp_now=tuple(cnt["sp%d" % i] for i in range(4))):
                wge(g, "t", xf)
                wge(g, "gd", pg)
                wge(g, "v", fv)
                wge(g, "s", fs)
                for i in range(4):
                    wge(g, "sp%d" % i, sp_now[i])
                for q in range(4):
                    spn = "sp%d" % q
                    for ic in range(q * 8, (q + 1) * 8):
                        inc(g.dma_start(out=x2tth[:, ic * 512:(ic + 1) * 512], in_=x2Th_sc[t, ic]), spn, 16)
                        inc(g.dma_start(out=x2ttl[:, ic * 512:(ic + 1) * 512], in_=x2Tl_sc[t, ic]), spn, 16)
                    x2q_marks[q] = cnt[spn]
            emit("gpsimd", _ldx2)

            h_use = []
            hev_v = []
            for er in range(4):
                for half in range(2):
                    nn = er * 2 + half

                    def _ldh(sync, er=er, half=half, nn=nn, ft=cnt["t"], fv=cnt["v"], fs=cnt["s"],
                             wr=w_ready_all):
                        if nn >= 2:
                            wge(sync, "t", h_use[nn - 2])
                        else:
                            wge(sync, "t", ft)
                            wge(sync, "v", fv)
                            wge(sync, "s", fs)
                            wge(sync, "g", wr)
                        par = "lda" if nn % 2 == 0 else "ldb"
                        inc(sync.dma_start(out=ldnrh[:, (nn % 2) * 2048:(nn % 2) * 2048 + 2048]
                                           .rearrange("p (a c) -> p a c", c=128),
                                           in_=ldh_sc[er, half * 16:(half + 1) * 16].transpose([1, 0, 2])), par, 16)
                        inc(sync.dma_start(out=ldnrl[:, (nn % 2) * 2048:(nn % 2) * 2048 + 2048]
                                           .rearrange("p (a c) -> p a c", c=128),
                                           in_=ldl_sc[er, half * 16:(half + 1) * 16].transpose([1, 0, 2])), par, 16)
                    emit("sync", _ldh)
                    ldh_sd = cnt["lda" if nn % 2 == 0 else "ldb"]

                    sl_h = ldnrh[:, (nn % 2) * 2048:(nn % 2) * 2048 + 2048]
                    sl_l = ldnrl[:, (nn % 2) * 2048:(nn % 2) * 2048 + 2048]

                    def _hmm(tensor, er=er, half=half, nn=nn, bc=b_clear_s,
                             sl_h=sl_h, sl_l=sl_l, ldh_sd=ldh_sd):
                        wge(tensor, "lda" if nn % 2 == 0 else "ldb", ldh_sd)
                        wge(tensor, "sp%d" % (half * 2), x2q_marks[half * 2])
                        wge(tensor, "sp%d" % (half * 2 + 1), x2q_marks[half * 2 + 1])
                        if nn == 0:
                            wge(tensor, "s", bc)
                        if half == 0 and er >= 2:
                            wge(tensor, "v", hev_v[er - 2])
                        for i2 in range(16):
                            ic = half * 16 + i2
                            mm3(tensor, bank(er),
                                sl_h[:, i2 * 128:(i2 + 1) * 128], sl_l[:, i2 * 128:(i2 + 1) * 128],
                                x2tth[:, ic * 512:(ic + 1) * 512], x2ttl[:, ic * 512:(ic + 1) * 512],
                                ic == 0, ic == NIC - 1)
                        inc(tensor.last, "t")
                    emit("tensor", _hmm)
                    h_use.append(cnt["t"])

                def _hev(vector, er=er, mk=h_use[-1], cv=cer_v, ps=cnt["s"]):
                    wge(vector, "t", mk)
                    wge(vector, "v", cv)
                    wge(vector, "s", ps)
                    vector.tensor_scalar(hsf[:], bank(er), cer[:, er:er + 1], None, ALU.mult)
                    vector.drain()
                    inc(vector.last, "v")
                emit("vector", _hev)
                hv = cnt["v"]
                split_hl(hsf[:],
                         hsh[:, er * 512:(er + 1) * 512],
                         hsl[:, er * 512:(er + 1) * 512],
                         waits_s=(("v", hv),))
                hev_v.append(cnt["v"])
            hs_done_v = cnt["v"]
            hs_done_s = cnt["s"]

            wb_use = []
            upm_marks = []
            ffn_first_s = [0] * 8
            stg_gd = [prev_stg_gd] * 8
            lup_splits = []

            for og in range(8):
                for ic in range(NIC):
                    ni = og * NIC + ic

                    def _wbl(sync, og=og, ic=ic, ni=ni, ft=cnt["t"], fv=cnt["v"], fs=cnt["s"],
                             wr=w_ready_all):
                        if ni >= 2:
                            wge(sync, "t", wb_use[ni - 2])
                        else:
                            wge(sync, "t", ft)
                            wge(sync, "v", fv)
                            wge(sync, "s", fs)
                            wge(sync, "g", wr)
                        par = "wba" if ni % 2 == 0 else "wbb"
                        inc(sync.dma_start(out=wringbh[:, (ni % 2) * 512:(ni % 2) * 512 + 512],
                                           in_=wbh_sc[og, ic]), par, 16)
                        inc(sync.dma_start(out=wringbl[:, (ni % 2) * 512:(ni % 2) * 512 + 512],
                                           in_=wbl_sc[og, ic]), par, 16)
                    emit("sync", _wbl)
                    wsd = cnt["wba" if ni % 2 == 0 else "wbb"]

                    sl_h = wringbh[:, (ni % 2) * 512:(ni % 2) * 512 + 512]
                    sl_l = wringbl[:, (ni % 2) * 512:(ni % 2) * 512 + 512]

                    def _wbm(tensor, og=og, ic=ic, ni=ni, hv=hs_done_v, hs=hs_done_s,
                             sl_h=sl_h, sl_l=sl_l, wsd=wsd):
                        wge(tensor, "wba" if ni % 2 == 0 else "wbb", wsd)
                        if ni == 0:
                            wge(tensor, "v", hv)
                            wge(tensor, "s", hs)
                        if ic == 0 and og >= 2:
                            wge(tensor, "s", ffn_first_s[og - 2])
                        pb = [0, 1, 2, 3] if og % 2 == 0 else [4, 5, 6, 7]
                        for tc in range(4):
                            mm3(tensor, bank(pb[tc]),
                                x2tth[:, ic * 512 + tc * 128: ic * 512 + tc * 128 + 128],
                                x2ttl[:, ic * 512 + tc * 128: ic * 512 + tc * 128 + 128],
                                sl_h, sl_l, ic == 0, False)
                        inc(tensor.last, "t")
                    emit("tensor", _wbm)
                    wb_use.append(cnt["t"])

                for er in range(4):
                    un = og * 4 + er

                    def _lupl(sync, og=og, er=er, un=un, ft=cnt["t"], fv=cnt["v"], fs=cnt["s"],
                              wr=w_ready_all):
                        if un >= 2:
                            wge(sync, "t", lup_splits[un - 2][2])
                        else:
                            wge(sync, "t", ft)
                            wge(sync, "v", fv)
                            wge(sync, "s", fs)
                            wge(sync, "g", wr)
                        par = "lua" if un % 2 == 0 else "lub"
                        inc(sync.dma_start(out=luprh[:, (un % 2) * 512:(un % 2) * 512 + 512],
                                           in_=luh_sc[og, er]), par, 16)
                        inc(sync.dma_start(out=luprl[:, (un % 2) * 512:(un % 2) * 512 + 512],
                                           in_=lul_sc[og, er]), par, 16)
                    emit("sync", _lupl)
                    usd = cnt["lua" if un % 2 == 0 else "lub"]

                    usl_h = luprh[:, (un % 2) * 512:(un % 2) * 512 + 512]
                    usl_l = luprl[:, (un % 2) * 512:(un % 2) * 512 + 512]

                    def _upm(tensor, og=og, er=er, un=un, usd=usd,
                             usl_h=usl_h, usl_l=usl_l):
                        wge(tensor, "lua" if un % 2 == 0 else "lub", usd)
                        pb = [0, 1, 2, 3] if og % 2 == 0 else [4, 5, 6, 7]
                        for tc in range(4):
                            mm3(tensor, bank(pb[tc]),
                                hsh[:, er * 512 + tc * 128: er * 512 + tc * 128 + 128],
                                hsl[:, er * 512 + tc * 128: er * 512 + tc * 128 + 128],
                                usl_h, usl_l, False, er == 3)
                        inc(tensor.last, "t")
                    emit("tensor", _upm)
                    lup_splits.append((smk, vmk, cnt["t"]))
                upm_marks.append(cnt["t"])

                def _fev(scalar, og=og, upm_t=upm_marks[-1], sgd=stg_gd[og - 1] if og > 0 else stg_gd[7]):
                    wge(scalar, "t", upm_t)
                    wge(scalar, "g", sgd)
                    pb = [0, 1, 2, 3] if og % 2 == 0 else [4, 5, 6, 7]
                    for tc in range(4):
                        scalar.copy(stg[:, tc * 512:(tc + 1) * 512], bank(pb[tc]))
                        scalar.drain()
                    inc(scalar.last, "s")
                emit("scalar", _fev)
                ffn_first_s[og] = cnt["s"]

                def _fsp(g, t=t, og=og, mk=ffn_first_s[og]):
                    wge(g, "s", mk)
                    for tc in range(4):
                        inc(g.dma_start(out=ffn_sc[t * 4 + tc, :, og * 512:(og + 1) * 512],
                                        in_=stg[:, tc * 512:(tc + 1) * 512]), "gd", 16)
                    gflush(g)
                emit("gpsimd", _fsp)
                stg_gd[og] = cnt["g"]

            prev_stg_gd = stg_gd[7]
            x2tt_free = cnt["t"]
            ffn_done_gd = cnt["gd"]
            b_clear_s = cnt["s"]

            for tc in range(4):
                ch = t * 4 + tc

                def _ldf(g, ch=ch, pv=cnt["v"], pg=cnt["gd"]):
                    wge(g, "v", pv)
                    wge(g, "gd", pg)
                    inc(g.dma_start(out=ffb[:], in_=ffn_sc[ch]), "gd", 16)
                    inc(g.dma_start(out=rrb[:], in_=res1_sc[ch]), "gd", 16)
                    gflush(g)
                emit("gpsimd", _ldf)
                ldf_gd = cnt["g"]

                lnf = ln_chunk(ffb[:], ffb[:], base=24, waits=(("g", ldf_gd),))

                def _fin(vector):
                    vector.tensor_tensor(ffb[:], ffb[:], rrb[:], ALU.add)
                    vector.drain()
                    inc(vector.last, "v")
                emit("vector", _fin)
                fin_v = cnt["v"]

                def _outd(g, t=t, tc=tc, fin_v=fin_v):
                    wge(g, "v", fin_v)
                    inc(g.dma_start(out=out[t * 512 + tc * 128: t * 512 + (tc + 1) * 128, :], in_=ffb[:]), "gd", 16)
                emit("gpsimd", _outd)
                prev_out_gd = cnt["gd"]

            st_b.close()

        def _fini(g):
            g.wait_ge(sems["gd"], cnt["gd"])
        emit("gpsimd", _fini)

    return nc



def _rope_cache():
    inv_freq = 1.0 / 10000.0 ** (np.arange(0, DH, 2, dtype=np.float32) / DH)
    t = np.arange(S, dtype=np.float32)
    f = np.outer(t, inv_freq)
    cos = np.zeros((S, DH), dtype=np.float32)
    sin = np.zeros((S, DH), dtype=np.float32)
    cos[:, 0::2] = np.cos(f)
    cos[:, 1::2] = np.cos(f)
    sin[:, 0::2] = np.sin(f)
    sin[:, 1::2] = np.sin(f)
    return cos, sin


def _stage_inputs(L0_attn_x, L0_attn_Wq, L0_attn_Wk, L0_attn_Wv, L0_attn_Wo,
                  L0_ffn_W_org, L0_ffn_lora_down, L0_ffn_lora_up, L0_ffn_lora_keys, L0_ffn_Wq):
    f32 = np.float32
    cos, sin = _rope_cache()

    wqT = np.ascontiguousarray(np.asarray(L0_attn_Wq).T)
    woT = np.ascontiguousarray(np.asarray(L0_attn_Wo).T)
    worgT = np.ascontiguousarray(np.asarray(L0_ffn_W_org).T)
    wkT = np.ascontiguousarray(np.asarray(L0_attn_Wk).T)
    wvT = np.ascontiguousarray(np.asarray(L0_attn_Wv).T)

    wq_t = np.ascontiguousarray(wqT.reshape(NIC, 128, H, 128).transpose(2, 0, 1, 3))
    wo_t = np.ascontiguousarray(woT.reshape(NIC, 128, 8, 512).transpose(2, 0, 1, 3))
    worg_t = np.ascontiguousarray(worgT.reshape(NIC, 128, 8, 512).transpose(2, 0, 1, 3))
    wkv_t = np.ascontiguousarray(np.concatenate(
        [wkT.reshape(NIC, 128, 128), wvT.reshape(NIC, 128, 128)], axis=2))

    ld = np.asarray(L0_ffn_lora_down)
    ldn_full = np.concatenate([ld[e] for e in range(NE)], axis=1)
    ldn_t = np.ascontiguousarray(ldn_full.reshape(NIC, 128, 4, 128).transpose(2, 0, 1, 3))
    lu = np.asarray(L0_ffn_lora_up).reshape(NE * RR, D)
    lup_t = np.ascontiguousarray(lu.reshape(4, 128, 8, 512).transpose(2, 0, 1, 3))

    prot = np.zeros((128, 128), f32)
    for i in range(64):
        prot[i, i + 64] = -1.0
    for i in range(64, 128):
        prot[i, i - 64] = 1.0
    protT = np.ascontiguousarray(prot.T)

    ident = np.eye(128, dtype=f32)
    ones_r = np.ones((1, 512), f32)
    constc = np.zeros((128, 4), f32)
    constc[:, 0] = EPS
    constc[:, 1] = 1.0
    constc[:, 2] = 0.0
    constc[:, 3] = 1.0 / D

    x_full = np.asarray(L0_attn_x)

    in_maps = []
    for c in range(NCORE):
        b = c // 4
        q0 = (c % 4) * CORE_T
        lo, hi = q0 - HALO, q0 + CORE_T + HALO
        shard = np.zeros((SH_R, D), f32)
        slo, shi = max(lo, 0), min(hi, S)
        shard[slo - lo: shi - lo] = x_full[b, slo:shi]

        pos = np.clip(np.arange(lo, hi), 0, S - 1)
        cosT_s = np.ascontiguousarray(cos[pos].T)
        sinT_s = np.ascontiguousarray(sin[pos].T)

        maskr = np.zeros((1, 512), f32)
        if lo < 0:
            maskr[0, 0:256] = MASK_NEG
        if hi > S:
            maskr[0, 256:512] = MASK_NEG

        alp = np.zeros((128, 16), f32)
        for j in range(16):
            offs = (q0 + 128 * j + np.arange(128)) % 256
            alp[:, j] = offs / 255.0
            if (q0 + 128 * j) // 256 == 0:
                alp[:, j] = 1.0
        alpm = 1.0 - alp

        bm_t = np.zeros((1, NB), f32)
        bm_t[0, b] = 1.0
        bm_p = np.zeros((NB, 1), f32)
        bm_p[b, 0] = 1.0

        in_maps.append({
            "xs": shard, "wq": wq_t, "wo": wo_t, "worg": worg_t, "wkv": wkv_t,
            "ldn": ldn_t, "lup": lup_t, "wqf": np.ascontiguousarray(np.asarray(L0_ffn_Wq)),
            "keysT": np.ascontiguousarray(np.asarray(L0_ffn_lora_keys).T),
            "cosT": cosT_s, "sinT": sinT_s, "protI": protT, "identf": ident,
            "alpha_c": alp, "alpham_c": alpm, "mask_r": maskr, "ones_r": ones_r,
            "bmaskT": bm_t, "bmaskP": bm_p, "constc": constc,
        })
    return in_maps


def kernel(**inputs):
    in_maps = _stage_inputs(**inputs)
    nc = build_program()
    ncore_run = int(os.environ.get("KNCORE", str(NCORE)))
    res = run_bass_kernel_spmd(nc, in_maps[:ncore_run], core_ids=list(range(ncore_run)))
    if ncore_run < NCORE:
        full = np.zeros((NB, S, D), np.float32)
        for c in range(ncore_run):
            full[c // 4, (c % 4) * CORE_T:(c % 4) * CORE_T + CORE_T] = res.results[c]["out"]
        return full
    outs = [res.results[c]["out"] for c in range(NCORE)]
    full = np.empty((NB, S, D), np.float32)
    for c in range(NCORE):
        b = c // 4
        q0 = (c % 4) * CORE_T
        full[b, q0:q0 + CORE_T] = outs[c]
    return full


# version 12 (final)
# speedup vs baseline: 1.3755x; 1.0027x over previous
import math
import os
import contextlib
import numpy as np

import concourse.bass as bass
import concourse.mybir as mybir
from concourse.bass_utils import run_bass_kernel_spmd

F32 = mybir.dt.float32
BF16 = mybir.dt.bfloat16
AF = mybir.ActivationFunctionType
ALU = mybir.AluOpType
AX = mybir.AxisListType

D = 4096
H = 32
DH = 128
S = 8192
NB = 2
NCORE = 8
CORE_T = 2048
HALO = 256
SH_R = CORE_T + 2 * HALO
TQ = 512
NT = CORE_T // TQ
NIC = D // 128
SCALE = 1.0 / math.sqrt(DH)
EPS = 1e-5
NE, RR, KR = 8, 64, 128
ROUTE_C = 1.0 / (S * math.sqrt(KR))
NKC = SH_R // 128
MASK_NEG = -1e9
STAGE = int(os.environ.get("KSTAGE", "9"))
KSUB = os.environ.get("KSUB", "all")


def build_program():
    nc = bass.Bass("TRN2", target_bir_lowering=False)

    xs = nc.dram_tensor("xs", [SH_R, D], F32, kind="ExternalInput").ap()
    wq = nc.dram_tensor("wq", [H, NIC, 128, 128], F32, kind="ExternalInput").ap()
    wo = nc.dram_tensor("wo", [8, NIC, 128, 512], F32, kind="ExternalInput").ap()
    worg = nc.dram_tensor("worg", [8, NIC, 128, 512], F32, kind="ExternalInput").ap()
    wkv = nc.dram_tensor("wkv", [NIC, 128, 256], F32, kind="ExternalInput").ap()
    ldn = nc.dram_tensor("ldn", [4, NIC, 128, 128], F32, kind="ExternalInput").ap()
    lup = nc.dram_tensor("lup", [8, 4, 128, 512], F32, kind="ExternalInput").ap()
    wqf = nc.dram_tensor("wqf", [KR, D], F32, kind="ExternalInput").ap()
    keysT = nc.dram_tensor("keysT", [KR, NE], F32, kind="ExternalInput").ap()
    cosT = nc.dram_tensor("cosT", [128, SH_R], F32, kind="ExternalInput").ap()
    sinT = nc.dram_tensor("sinT", [128, SH_R], F32, kind="ExternalInput").ap()
    protI = nc.dram_tensor("protI", [128, 128], F32, kind="ExternalInput").ap()
    identf = nc.dram_tensor("identf", [128, 128], F32, kind="ExternalInput").ap()
    alpha_c = nc.dram_tensor("alpha_c", [128, 16], F32, kind="ExternalInput").ap()
    alpham_c = nc.dram_tensor("alpham_c", [128, 16], F32, kind="ExternalInput").ap()
    mask_r = nc.dram_tensor("mask_r", [1, 512], F32, kind="ExternalInput").ap()
    ones_r = nc.dram_tensor("ones_r", [1, 512], F32, kind="ExternalInput").ap()
    bmaskT = nc.dram_tensor("bmaskT", [1, NB], F32, kind="ExternalInput").ap()
    bmaskP = nc.dram_tensor("bmaskP", [NB, 1], F32, kind="ExternalInput").ap()
    constc = nc.dram_tensor("constc", [128, 4], F32, kind="ExternalInput").ap()

    out = nc.dram_tensor("out", [CORE_T, D], F32, kind="ExternalOutput").ap()

    xT1h_sc = nc.dram_tensor("xT1h_sc", [NT, NIC, 128, TQ], BF16)
    xT1l_sc = nc.dram_tensor("xT1l_sc", [NT, NIC, 128, TQ], BF16)
    res1_sc = nc.dram_tensor("res1_sc", [16, 128, D], F32)
    x2Th_sc = nc.dram_tensor("x2Th_sc", [NT, NIC, 128, TQ], BF16)
    x2Tl_sc = nc.dram_tensor("x2Tl_sc", [NT, NIC, 128, TQ], BF16)
    ffn_sc = nc.dram_tensor("ffn_sc", [16, 128, D], F32)
    cc_in = nc.dram_tensor("cc_in", [NB, NE], F32)
    cc_out = nc.dram_tensor("cc_out", [NB, NE], F32)
    wqh_sc = nc.dram_tensor("wqh_sc", [H, NIC, 128, 128], BF16)
    wql_sc = nc.dram_tensor("wql_sc", [H, NIC, 128, 128], BF16)
    woh_sc = nc.dram_tensor("woh_sc", [8, NIC, 128, 512], BF16)
    wol_sc = nc.dram_tensor("wol_sc", [8, NIC, 128, 512], BF16)
    wbh_sc = nc.dram_tensor("wbh_sc", [8, NIC, 128, 512], BF16)
    wbl_sc = nc.dram_tensor("wbl_sc", [8, NIC, 128, 512], BF16)
    ldh_sc = nc.dram_tensor("ldh_sc", [4, NIC, 128, 128], BF16)
    ldl_sc = nc.dram_tensor("ldl_sc", [4, NIC, 128, 128], BF16)
    luh_sc = nc.dram_tensor("luh_sc", [8, 4, 128, 512], BF16)
    lul_sc = nc.dram_tensor("lul_sc", [8, 4, 128, 512], BF16)

    es = contextlib.ExitStack()
    with es:
        sems, cnt = {}, {}
        for name in ("t", "v", "s", "g", "sd", "gd", "cc",
                     "pre", "kap", "wkv", "cs",
                     "wqa", "wqb", "woa", "wob", "wba", "wbb",
                     "lda", "ldb", "lua", "lub",
                     "wla", "wlb", "sp0", "sp1", "sp2", "sp3"):
            sems[name] = es.enter_context(nc.semaphore("sem_" + name))
            cnt[name] = 0

        block = es.enter_context(nc.Block())

        class EngProxy:
            def __init__(self, eng):
                object.__setattr__(self, "_e", eng)
                object.__setattr__(self, "last", None)

            def __getattr__(self, n):
                a = getattr(self._e, n)
                if callable(a) and n not in ("drain", "wait_ge"):
                    def w(*args, **kw):
                        r = a(*args, **kw)
                        if n not in ("nop",):
                            object.__setattr__(self, "last", r)
                        return r
                    return w
                return a

        def emit(engname, fn):
            getattr(block, engname)(lambda e: fn(EngProxy(e)))

        def inc(instr, name, n=1):
            instr.then_inc(sems[name], n)
            cnt[name] += n
            return cnt[name]

        def wge(eng, name, val):
            if val > 0:
                eng.wait_ge(sems[name], val)

        def gflush(g):
            g.wait_ge(sems["gd"], cnt["gd"])
            inc(g.engine_nop(), "g")
            return cnt["g"]

        krTh = es.enter_context(nc.sbuf_tensor("krTh", [128, SH_R], BF16))
        krTl = es.enter_context(nc.sbuf_tensor("krTl", [128, SH_R], BF16))
        vaugh = es.enter_context(nc.sbuf_tensor("vaugh", [128, NKC * 129], BF16))
        vaugl = es.enter_context(nc.sbuf_tensor("vaugl", [128, NKC * 129], BF16))
        protT = es.enter_context(nc.sbuf_tensor("protT", [128, 128], F32))
        idf = es.enter_context(nc.sbuf_tensor("idf", [128, 128], F32))
        alpt = es.enter_context(nc.sbuf_tensor("alpt", [128, 16], F32))
        alpmt = es.enter_context(nc.sbuf_tensor("alpmt", [128, 16], F32))
        maskt = es.enter_context(nc.sbuf_tensor("maskt", [1, 512], F32))
        onest = es.enter_context(nc.sbuf_tensor("onest", [1, 512], F32))
        bmt = es.enter_context(nc.sbuf_tensor("bmt", [1, NB], F32))
        bmp = es.enter_context(nc.sbuf_tensor("bmp", [NB, 1], F32))
        cct = es.enter_context(nc.sbuf_tensor("cct", [128, 4], F32))
        kaph = es.enter_context(nc.sbuf_tensor("kaph", [128, NIC * NE], BF16))
        kapl = es.enter_context(nc.sbuf_tensor("kapl", [128, NIC * NE], BF16))
        zbuf = es.enter_context(nc.sbuf_tensor("zbuf", [128, NIC * 16], F32))
        z32 = es.enter_context(nc.sbuf_tensor("z32", [128, NIC], F32))
        z32h = es.enter_context(nc.sbuf_tensor("z32h", [128, NIC], BF16))
        z32l = es.enter_context(nc.sbuf_tensor("z32l", [128, NIC], BF16))
        coefs = es.enter_context(nc.sbuf_tensor("coefs", [1, NE], F32))
        routs = es.enter_context(nc.sbuf_tensor("routs", [1, NE], F32))
        rout2 = es.enter_context(nc.sbuf_tensor("rout2", [NB, NE], F32))
        routp = es.enter_context(nc.sbuf_tensor("routp", [NB, NE], F32))
        bc8 = es.enter_context(nc.sbuf_tensor("bc8", [128, NE], F32))
        cer = es.enter_context(nc.sbuf_tensor("cer", [128, 4], F32))
        smalls = es.enter_context(nc.sbuf_tensor("smalls", [128, 72], F32))
        costt = es.enter_context(nc.sbuf_tensor("costt", [128, 512], F32))
        sintt = es.enter_context(nc.sbuf_tensor("sintt", [128, 512], F32))
        dumpb = es.enter_context(nc.sbuf_tensor("dumpb", [128, D], BF16))
        protB = es.enter_context(nc.sbuf_tensor("protB", [128, 128], BF16))
        maskB = es.enter_context(nc.sbuf_tensor("maskB", [1, 512], BF16))
        onesB = es.enter_context(nc.sbuf_tensor("onesB", [1, 512], BF16))
        bmtB = es.enter_context(nc.sbuf_tensor("bmtB", [1, NB], BF16))
        bmpB = es.enter_context(nc.sbuf_tensor("bmpB", [NB, 1], BF16))
        routsh = es.enter_context(nc.sbuf_tensor("routsh", [1, NE], BF16))
        routsl = es.enter_context(nc.sbuf_tensor("routsl", [1, NE], BF16))
        rout2h = es.enter_context(nc.sbuf_tensor("rout2h", [NB, NE], BF16))
        rout2l = es.enter_context(nc.sbuf_tensor("rout2l", [NB, NE], BF16))
        coefsh = es.enter_context(nc.sbuf_tensor("coefsh", [1, NE], BF16))
        coefsl = es.enter_context(nc.sbuf_tensor("coefsl", [1, NE], BF16))

        psA = es.enter_context(nc.psum_tensor("psA", [128, 1536], F32))
        psB = es.enter_context(nc.psum_tensor("psB", [128, 1536], F32))
        psC = es.enter_context(nc.psum_tensor("psC", [128, 512], F32))
        psD = es.enter_context(nc.psum_tensor("psD", [128, 512], F32))
        bank_list = [(psA, 0), (psA, 512), (psA, 1024), (psB, 0), (psB, 512), (psB, 1024), (psC, 0), (psD, 0)]

        def bank(i, w=512):
            tns, c = bank_list[i % 8]
            return tns[:, c:c + w]

        def mm3(tensor, out_ap, lh, ll, rh, rl, start, stop):
            tensor.matmul(out_ap, lh, rh, start=start, stop=False)
            tensor.matmul(out_ap, lh, rl, start=False, stop=False)
            tensor.matmul(out_ap, ll, rh, start=False, stop=stop)

        def _pre(sync):
            inc(sync.dma_start(out=protT[:], in_=protI[:]), "pre", 16)
            inc(sync.dma_start(out=idf[:], in_=identf[:]), "pre", 16)
            inc(sync.dma_start(out=alpt[:], in_=alpha_c[:]), "pre", 16)
            inc(sync.dma_start(out=alpmt[:], in_=alpham_c[:]), "pre", 16)
            inc(sync.dma_start(out=maskt[:], in_=mask_r[:]), "pre", 16)
            inc(sync.dma_start(out=onest[:], in_=ones_r[:]), "pre", 16)
            inc(sync.dma_start(out=bmt[:], in_=bmaskT[:]), "pre", 16)
            inc(sync.dma_start(out=bmp[:], in_=bmaskP[:]), "pre", 16)
            inc(sync.dma_start(out=cct[:], in_=constc[:]), "pre", 16)
        emit("sync", _pre)
        pre_sd = cnt["pre"]

        nc.const_aps.register(F32, 0.0, cct[:, 2:3])
        nc.const_aps.register(F32, 1.0, cct[:, 1:2])

        def _preb(scalar):
            wge(scalar, "pre", pre_sd)
            scalar.copy(protB[:], protT[:])
            scalar.drain()
            scalar.copy(maskB[:], maskt[:])
            scalar.drain()
            scalar.copy(onesB[:], onest[:])
            scalar.drain()
            scalar.copy(bmtB[:], bmt[:])
            scalar.drain()
            scalar.copy(bmpB[:], bmp[:])
            scalar.drain()
            inc(scalar.last, "s")
        emit("scalar", _preb)
        preb_s = cnt["s"]

        def split_hl(src_ap, hi_ap, lo_ap, waits_s=(), waits_v=()):
            def _v(vector):
                seen = set()
                for nm, vl in tuple(waits_s) + tuple(waits_v):
                    if (nm, vl) not in seen:
                        seen.add((nm, vl))
                        wge(vector, nm, vl)
                vector.tensor_scalar(hi_ap, src_ap, 1.0, None, ALU.mult)
                vector.drain()
                vector.tensor_tensor(lo_ap, src_ap, hi_ap, ALU.subtract)
                vector.drain()
                inc(vector.last, "v")
            emit("vector", _v)
            return cnt["s"], cnt["v"]

        st_kap = contextlib.ExitStack()
        kwq = st_kap.enter_context(nc.sbuf_tensor("kwq", [128, D], F32))
        kwqh = st_kap.enter_context(nc.sbuf_tensor("kwqh", [128, D], BF16))
        kwql = st_kap.enter_context(nc.sbuf_tensor("kwql", [128, D], BF16))
        kktf = st_kap.enter_context(nc.sbuf_tensor("kktf", [KR, NE], F32))
        kkth = st_kap.enter_context(nc.sbuf_tensor("kkth", [KR, NE], BF16))
        kktl = st_kap.enter_context(nc.sbuf_tensor("kktl", [KR, NE], BF16))
        kapf = st_kap.enter_context(nc.sbuf_tensor("kapf", [128, NIC * NE], F32))

        def _kap_ld(sync):
            inc(sync.dma_start(out=kwq[:], in_=wqf[:]), "kap", 16)
            inc(sync.dma_start(out=kktf[:], in_=keysT[:]), "kap", 16)
        emit("sync", _kap_ld)
        kap_sd = cnt["kap"]

        split_hl(kwq[:], kwqh[:], kwql[:], waits_s=(("kap", kap_sd),), waits_v=(("kap", kap_sd),))
        split_hl(kktf[:], kkth[:], kktl[:])
        kap_split_v = cnt["v"]
        kap_split_s = cnt["s"]

        def _kap_mm(tensor):
            wge(tensor, "v", kap_split_v)
            wge(tensor, "s", kap_split_s)
            nmm = NIC if KSUB == "all" else (0 if KSUB == "nomm" else int(KSUB))
            for ic in range(nmm):
                mm3(tensor, psA[:, ic * NE:(ic + 1) * NE],
                    kwqh[:, ic * 128:(ic + 1) * 128], kwql[:, ic * 128:(ic + 1) * 128],
                    kkth[:], kktl[:], True, True)
            inc(tensor.last, "t")
        emit("tensor", _kap_mm)
        kap_t = cnt["t"]

        def _kap_ev(vector):
            wge(vector, "t", kap_t)
            inc(vector.tensor_copy(kapf[:], psA[:, 0:NIC * NE]), "v")
            vector.drain()
        emit("vector", _kap_ev)
        split_hl(kapf[:], kaph[:], kapl[:], waits_s=(("v", cnt["v"]),))
        kap_v = cnt["v"]
        kap_s = cnt["s"]
        kap_t_f = cnt["t"]
        st_kap.close()

        def ln_chunk(buf, dst, base, waits=()):
            sm = smalls[:, base:base + 1]
            sq = smalls[:, base + 1:base + 2]
            m = smalls[:, base + 2:base + 3]
            var = smalls[:, base + 3:base + 4]

            def _s(scalar):
                for nm, vl in waits:
                    wge(scalar, nm, vl)
                scalar.activation(dumpb[:], buf, AF.Square, accum_out=sq)
                scalar.drain()
                inc(scalar.last, "s")
            emit("scalar", _s)
            sq_s = cnt["s"]

            def _v(vector):
                for nm, vl in waits:
                    wge(vector, nm, vl)
                vector.reduce_sum(sm, buf, axis=AX.X)
                vector.drain()
                wge(vector, "s", sq_s)
                vector.tensor_scalar(m, sm, 1.0 / D, None, ALU.mult)
                vector.drain()
                vector.tensor_scalar(smalls[:, 65:66], m, m, None, ALU.mult)
                vector.drain()
                vector.scalar_tensor_tensor(var, sq, 1.0 / D, smalls[:, 65:66], ALU.mult, ALU.subtract)
                vector.drain()
                inc(vector.last, "v")
            emit("vector", _v)
            var_v = cnt["v"]

            def _s2(scalar):
                wge(scalar, "v", var_v)
                scalar.activation(smalls[:, 66:67], var, AF.Ln, bias=cct[:, 0:1], scale=1.0)
                scalar.drain()
                scalar.activation(sq, smalls[:, 66:67], AF.Exp, scale=-0.5)
                scalar.drain()
                inc(scalar.last, "s")
            emit("scalar", _s2)
            rs_s = cnt["s"]

            def _v2(vector):
                wge(vector, "s", rs_s)
                vector.tensor_scalar(sm, m, sq, -1.0, ALU.mult, ALU.mult)
                vector.drain()
                vector.tensor_scalar(dst, buf, sq, sm, ALU.mult, ALU.add)
                vector.drain()
                inc(vector.last, "v")
            emit("vector", _v2)
            return cnt["v"]

        def transpose_split(srcs, dsts_h, dsts_l, first_waits=(), psum_off=0):
            ev_s = []
            ev_v = []
            for n, (sp, dh, dl) in enumerate(zip(srcs, dsts_h, dsts_l)):
                sl = bank(psum_off + n % 4, 128)

                def _t(tensor, sp=sp, sl=sl, n=n):
                    if n == 0:
                        for nm, vl in first_waits:
                            wge(tensor, nm, vl)
                    if n >= 4:
                        wge(tensor, "s", ev_s[n - 4])
                        wge(tensor, "v", ev_v[n - 4])
                    tensor.transpose(sl, sp, idf[:])
                    inc(tensor.last, "t")
                emit("tensor", _t)
                tmark = cnt["t"]

                def _v(vector, dl=dl, dh=dh, sl=sl, tmark=tmark):
                    wge(vector, "t", tmark)
                    vector.tensor_scalar(dh, sl, 1.0, None, ALU.mult)
                    vector.drain()
                    vector.tensor_tensor(dl, sl, dh, ALU.subtract)
                    vector.drain()
                    inc(vector.last, "v")
                emit("vector", _v)
                ev_s.append(cnt["s"])
                ev_v.append(cnt["v"])
            return cnt["s"], cnt["v"]

        if STAGE < 2:
            def _fini0(g):
                g.wait_ge(sems["gd"], cnt["gd"])
                g.wait_ge(sems["v"], cnt["v"])
                g.wait_ge(sems["s"], cnt["s"])
                g.wait_ge(sems["t"], cnt["t"])
            emit("gpsimd", _fini0)
            return nc
        p0 = contextlib.ExitStack()
        with p0:
            xsub = p0.enter_context(nc.sbuf_tensor("xsub", [128, 2 * D], F32))
            xT1sh = p0.enter_context(nc.sbuf_tensor("xT1sh", [128, NIC * 256], BF16))
            xT1sl = p0.enter_context(nc.sbuf_tensor("xT1sl", [128, NIC * 256], BF16))
            wkvf = p0.enter_context(nc.sbuf_tensor("wkvf", [128, NIC * 128], F32))
            wkvh = p0.enter_context(nc.sbuf_tensor("wkvh", [128, NIC * 256], BF16))
            wkvl = p0.enter_context(nc.sbuf_tensor("wkvl", [128, NIC * 256], BF16))
            kfm = p0.enter_context(nc.sbuf_tensor("kfm", [128, 256], F32))
            kraw = p0.enter_context(nc.sbuf_tensor("kraw", [128, 256], F32))
            krawh = p0.enter_context(nc.sbuf_tensor("krawh", [128, 256], BF16))
            krawl = p0.enter_context(nc.sbuf_tensor("krawl", [128, 256], BF16))
            ktmp = p0.enter_context(nc.sbuf_tensor("ktmp", [128, 256], F32))

            wkv_marks = []
            for wh in range(2):
                def _wkv_ld(sync, wh=wh, pm=wkv_marks[-1][1] if wkv_marks else 0):
                    if wh == 0:
                        wge(sync, "pre", pre_sd)
                        wge(sync, "v", kap_v)
                        wge(sync, "s", kap_s)
                    else:
                        wge(sync, "v", pm)
                    for ic in range(NIC):
                        inc(sync.dma_start(out=wkvf[:, ic * 128:(ic + 1) * 128],
                                           in_=wkv[ic, :, wh * 128:(wh + 1) * 128]), "wkv", 16)
                emit("sync", _wkv_ld)
                wsd = cnt["wkv"]
                hi3 = wkvh[:].rearrange("p (ic c) -> p ic c", c=256)[:, :, wh * 128:(wh + 1) * 128]
                lo3 = wkvl[:].rearrange("p (ic c) -> p ic c", c=256)[:, :, wh * 128:(wh + 1) * 128]
                src3 = wkvf[:].rearrange("p (ic c) -> p ic c", c=128)
                smk, vmk = split_hl(src3, hi3, lo3, waits_s=(("wkv", wsd),), waits_v=(("wkv", wsd),))
                wkv_marks.append((smk, vmk))
            wkv_s, wkv_v = wkv_marks[-1]

            xsub_free_t = 0
            xT1_free_t = 0
            cs_free_v = 0

            for hsi in range(SH_R // 256):
                r0 = hsi * 256

                def _ld(g, r0=r0, xft=xsub_free_t):
                    wge(g, "t", xft)
                    if r0 == 0:
                        wge(g, "v", wkv_v)
                    for tc in range(2):
                        inc(g.dma_start(out=xsub[:, tc * D:(tc + 1) * D],
                                        in_=xs[r0 + tc * 128:r0 + (tc + 1) * 128, :]), "gd", 16)
                    gflush(g)
                emit("gpsimd", _ld)
                ld_gd = cnt["g"]

                def _cs(sync, r0=r0, csf=cs_free_v):
                    wge(sync, "v", csf)
                    inc(sync.dma_start(out=costt[:, 0:256], in_=cosT[:, r0:r0 + 256]), "cs", 16)
                    inc(sync.dma_start(out=sintt[:, 0:256], in_=sinT[:, r0:r0 + 256]), "cs", 16)
                emit("sync", _cs)
                cs_sd = cnt["cs"]

                for tc in range(2):
                    ln_chunk(xsub[:, tc * D:(tc + 1) * D], xsub[:, tc * D:(tc + 1) * D],
                             base=tc * 8, waits=(("g", ld_gd),))
                ln_v = cnt["v"]

                srcs, dsth, dstl = [], [], []
                for tc in range(2):
                    for ic in range(NIC):
                        srcs.append(xsub[:, tc * D + ic * 128: tc * D + (ic + 1) * 128])
                        dsth.append(xT1sh[:, ic * 256 + tc * 128: ic * 256 + (tc + 1) * 128])
                        dstl.append(xT1sl[:, ic * 256 + tc * 128: ic * 256 + (tc + 1) * 128])
                fw = [("v", ln_v), ("t", xT1_free_t)]
                xT1_sdone, xT1_vdone = transpose_split(srcs, dsth, dstl, first_waits=tuple(fw))
                xsub_free_t = cnt["t"]

                if 256 <= r0 < SH_R - 256:
                    t_idx = (r0 - 256) // 512
                    half = ((r0 - 256) % 512) // 256

                    def _spill(g, t_idx=t_idx, half=half, mk=xT1_sdone, mkv=xT1_vdone):
                        wge(g, "s", mk)
                        wge(g, "v", mkv)
                        for ic in range(NIC):
                            inc(g.dma_start(out=xT1h_sc[t_idx, ic, :, half * 256:half * 256 + 256],
                                            in_=xT1sh[:, ic * 256:(ic + 1) * 256]), "gd", 16)
                            inc(g.dma_start(out=xT1l_sc[t_idx, ic, :, half * 256:half * 256 + 256],
                                            in_=xT1sl[:, ic * 256:(ic + 1) * 256]), "gd", 16)
                    emit("gpsimd", _spill)

                def _kmm(tensor, mk=xT1_sdone, mkv=xT1_vdone):
                    wge(tensor, "s", preb_s)
                    wge(tensor, "s", mk)
                    wge(tensor, "v", mkv)
                    wge(tensor, "v", wkv_v)
                    wge(tensor, "s", wkv_s)
                    for ic in range(NIC):
                        mm3(tensor, psC[:, 0:256],
                            wkvh[:, ic * 256:ic * 256 + 128], wkvl[:, ic * 256:ic * 256 + 128],
                            xT1sh[:, ic * 256:(ic + 1) * 256], xT1sl[:, ic * 256:(ic + 1) * 256],
                            ic == 0, ic == NIC - 1)
                    inc(tensor.last, "t")
                emit("tensor", _kmm)
                kmm_t = cnt["t"]

                smk, vmk = split_hl(psC[:, 0:256], krawh[:], krawl[:],
                                    waits_s=(("t", kmm_t),), waits_v=(("t", kmm_t),))
                kraw_s = smk
                kraw_v = vmk

                def _kc(vector, kmm_t=kmm_t, cs_sd=cs_sd):
                    wge(vector, "t", kmm_t)
                    wge(vector, "cs", cs_sd)
                    vector.tensor_tensor(kfm[:], psC[:, 0:256], costt[:, 0:256], ALU.mult)
                    vector.drain()
                    inc(vector.last, "v")
                emit("vector", _kc)
                kc_v = cnt["v"]

                def _krot(tensor, kraw_s=kraw_s, kraw_v=kraw_v, kc_v=kc_v):
                    wge(tensor, "s", kraw_s)
                    wge(tensor, "v", kraw_v)
                    wge(tensor, "v", kc_v)
                    tensor.matmul(psC[:, 0:256], protB[:], krawh[:], start=True, stop=False)
                    tensor.matmul(psC[:, 0:256], protB[:], krawl[:], start=False, stop=True)
                    inc(tensor.last, "t")
                emit("tensor", _krot)
                krot_t = cnt["t"]

                def _kr2(vector, krot_t=krot_t, r0=r0):
                    wge(vector, "t", krot_t)
                    vector.tensor_tensor(ktmp[:], psC[:, 0:256], sintt[:, 0:256], ALU.mult)
                    vector.drain()
                    vector.tensor_tensor(kraw[:], kfm[:], ktmp[:], ALU.add)
                    vector.drain()
                    inc(vector.last, "v")
                emit("vector", _kr2)
                kr_f_v = cnt["v"]
                split_hl(kraw[:], krTh[:, r0:r0 + 256], krTl[:, r0:r0 + 256],
                         waits_s=(("v", kr_f_v),))
                cs_free_v = cnt["v"]
                krot_done_v = cnt["v"]

                for tc in range(2):
                    def _vmm(tensor, tc=tc, kv=krot_done_v, pv=cnt["v"], ps=cnt["s"]):
                        if tc == 0:
                            wge(tensor, "v", kv)
                            wge(tensor, "s", ps)
                        else:
                            wge(tensor, "v", pv)
                        for ic in range(NIC):
                            mm3(tensor, psD[:, 0:128],
                                xT1sh[:, ic * 256 + tc * 128: ic * 256 + (tc + 1) * 128],
                                xT1sl[:, ic * 256 + tc * 128: ic * 256 + (tc + 1) * 128],
                                wkvh[:, ic * 256 + 128:(ic + 1) * 256],
                                wkvl[:, ic * 256 + 128:(ic + 1) * 256],
                                ic == 0, ic == NIC - 1)
                        inc(tensor.last, "t")
                    emit("tensor", _vmm)
                    vmm_t = cnt["t"]

                    kc = r0 // 128 + tc
                    smk, vmk = split_hl(psD[:, 0:128],
                                        vaugh[:, kc * 129: kc * 129 + 128],
                                        vaugl[:, kc * 129: kc * 129 + 128],
                                        waits_s=(("t", vmm_t),), waits_v=(("t", vmm_t),))

                    def _vone(vector, kc=kc):
                        vector.memset(vaugh[:, kc * 129 + 128: kc * 129 + 129], 1.0)
                        vector.drain()
                        vector.memset(vaugl[:, kc * 129 + 128: kc * 129 + 129], 0.0)
                        vector.drain()
                        inc(vector.last, "v")
                    emit("vector", _vone)
                xT1_free_t = cnt["t"]

            kv_ready_v = cnt["v"]
            kv_ready_s = cnt["s"]
            p0_cs_free = cs_free_v
        p0_gd = cnt["gd"]
        w_ready = {}
        stw = contextlib.ExitStack()
        with stw:
            wchk = stw.enter_context(nc.sbuf_tensor("wchk", [128, 2 * 2048], F32, side="right"))
            wchkh = stw.enter_context(nc.sbuf_tensor("wchkh", [128, 4 * 2048], BF16, side="right"))
            wchkl = stw.enter_context(nc.sbuf_tensor("wchkl", [128, 4 * 2048], BF16, side="right"))
            wn = 0
            spl_marks = []
            sp_rounds = [0, 0, 0, 0]
            wjobs = [
                ("wq", wq, wqh_sc, wql_sc, 16, 128, H * NIC // 16),
                ("wo", wo, woh_sc, wol_sc, 4, 512, 8 * NIC // 4),
                ("wb", worg, wbh_sc, wbl_sc, 4, 512, 8 * NIC // 4),
                ("ld", ldn, ldh_sc, ldl_sc, 16, 128, 4 * NIC // 16),
                ("lu", lup, luh_sc, lul_sc, 4, 512, 8 * 4 // 4),
            ]
            for jname, srcap, dsth_ap, dstl_ap, ntile, tcols, nchunks in wjobs:
                src_flat = srcap.rearrange("a b p c -> (a b) p c")
                dh_flat = dsth_ap.rearrange("a b p c -> (a b) p c")
                dl_flat = dstl_ap.rearrange("a b p c -> (a b) p c")
                for ci in range(nchunks):
                    par = "wla" if wn % 2 == 0 else "wlb"
                    spn = "sp%d" % (wn % 4)

                    def _wld(sync, jn=jname, ci=ci, wn=wn, par=par, ntile=ntile, tcols=tcols,
                             src_flat=src_flat, ft=kap_t_f, fv=kap_v, fs=kap_s):
                        if wn >= 2:
                            sm, vm = spl_marks[wn - 2]
                            wge(sync, "s", sm)
                            wge(sync, "v", vm)
                        else:
                            wge(sync, "t", ft)
                            wge(sync, "v", fv)
                            wge(sync, "s", fs)
                        inc(sync.dma_start(out=wchk[:, (wn % 2) * 2048:(wn % 2) * 2048 + 2048]
                                           .rearrange("p (a c) -> p a c", c=tcols),
                                           in_=src_flat[ci * ntile:(ci + 1) * ntile].transpose([1, 0, 2])),
                            par, 16)
                    emit("sync", _wld)
                    wld_mark = cnt[par]

                    hi_sl = wchkh[:, (wn % 4) * 2048:(wn % 4) * 2048 + 2048]
                    lo_sl = wchkl[:, (wn % 4) * 2048:(wn % 4) * 2048 + 2048]
                    ws = ((par, wld_mark), (spn, sp_rounds[wn % 4] * 32))
                    smk, vmk = split_hl(wchk[:, (wn % 2) * 2048:(wn % 2) * 2048 + 2048],
                                        hi_sl, lo_sl, waits_s=ws, waits_v=ws)
                    spl_marks.append((smk, vmk))

                    def _wsp(g, jn=jname, ci=ci, wn=wn, spn=spn, smk=smk, vmk=vmk,
                             hi_sl=hi_sl, lo_sl=lo_sl, ntile=ntile, tcols=tcols,
                             dh_flat=dh_flat, dl_flat=dl_flat):
                        wge(g, "s", smk)
                        wge(g, "v", vmk)
                        inc(g.dma_start(out=dh_flat[ci * ntile:(ci + 1) * ntile].transpose([1, 0, 2]),
                                        in_=hi_sl.rearrange("p (a c) -> p a c", c=tcols)), spn, 16)
                        inc(g.dma_start(out=dl_flat[ci * ntile:(ci + 1) * ntile].transpose([1, 0, 2]),
                                        in_=lo_sl.rearrange("p (a c) -> p a c", c=tcols)), spn, 16)
                    emit("gpsimd", _wsp)
                    sp_rounds[wn % 4] += 1
                    wn += 1

                def _wclose(g):
                    for i in range(4):
                        wge(g, "sp%d" % i, cnt["sp%d" % i])
                    inc(g.engine_nop(), "g")
                emit("gpsimd", _wclose)
                w_ready[jname] = cnt["g"]
            w_ready_all = cnt["g"]
            for k in list(w_ready):
                w_ready[k] = max(w_ready[k], 0)


        if STAGE < 3:
            def _fini0(g):
                g.wait_ge(sems["gd"], cnt["gd"])
                g.wait_ge(sems["v"], cnt["v"])
                g.wait_ge(sems["s"], cnt["s"])
                g.wait_ge(sems["t"], cnt["t"])
            emit("gpsimd", _fini0)
            return nc
        res1_gd = [0] * 16
        x2T_gd = [0] * NT
        csq_free = p0_cs_free
        xT1t_free_prev = 0
        y_free_prev_v = 0
        x2tp_prev_gd = 0
        route_red_v = 0

        for t in range(NT if STAGE >= 4 else 1):
            r0q = 256 + t * TQ
            banks_clear_s = cnt["s"]
            banks_clear_v = cnt["v"]

            st_q = contextlib.ExitStack()
            qrTh = st_q.enter_context(nc.sbuf_tensor(f"qrTh{t}", [128, H * 512], BF16))
            qrTl = st_q.enter_context(nc.sbuf_tensor(f"qrTl{t}", [128, H * 512], BF16))

            st_a2 = contextlib.ExitStack()
            xT1th = st_a2.enter_context(nc.sbuf_tensor(f"xT1th{t}", [128, NIC * 512], BF16))
            xT1tl = st_a2.enter_context(nc.sbuf_tensor(f"xT1tl{t}", [128, NIC * 512], BF16))
            wringqh = st_a2.enter_context(nc.sbuf_tensor(f"wringqh{t}", [128, 2 * 1024], BF16))
            wringql = st_a2.enter_context(nc.sbuf_tensor(f"wringql{t}", [128, 2 * 1024], BF16))
            qrah = st_a2.enter_context(nc.sbuf_tensor(f"qrah{t}", [128, 2 * 512], BF16))
            qral = st_a2.enter_context(nc.sbuf_tensor(f"qral{t}", [128, 2 * 512], BF16))
            qca = st_a2.enter_context(nc.sbuf_tensor(f"qca{t}", [128, 2 * 512], F32))
            qsa = st_a2.enter_context(nc.sbuf_tensor(f"qsa{t}", [128, 2 * 512], F32))

            def _csq(sync, r0q=r0q, csf=csq_free):
                wge(sync, "v", csf)
                inc(sync.dma_start(out=costt[:], in_=cosT[:, r0q:r0q + 512]), "cs", 16)
                inc(sync.dma_start(out=sintt[:], in_=sinT[:, r0q:r0q + 512]), "cs", 16)
            emit("sync", _csq)
            csq_sd = cnt["cs"]

            xq_marks = [0, 0, 0, 0]

            def _ldx(g, t=t, xf=xT1t_free_prev, fv=cnt["v"], fs=cnt["s"], pg=cnt["gd"],
                     sp_now=tuple(cnt["sp%d" % i] for i in range(4))):
                wge(g, "t", xf)
                wge(g, "v", fv)
                wge(g, "s", fs)
                wge(g, "gd", pg)
                for i in range(4):
                    wge(g, "sp%d" % i, sp_now[i])
                for q in range(4):
                    spn = "sp%d" % q
                    for ic in range(q * 8, (q + 1) * 8):
                        inc(g.dma_start(out=xT1th[:, ic * 512:(ic + 1) * 512], in_=xT1h_sc[t, ic]), spn, 16)
                        inc(g.dma_start(out=xT1tl[:, ic * 512:(ic + 1) * 512], in_=xT1l_sc[t, ic]), spn, 16)
                    xq_marks[q] = cnt[spn]
            emit("gpsimd", _ldx)

            q_use = []
            rope_raw_s = []
            rope_rot_t = []
            rope_done_v = []
            wq_split_s = []
            wq_split_v = []

            def emit_rope(hd):
                qsl = slice((hd % 2) * 512, (hd % 2) * 512 + 512)
                ws = (("t", q_use[hd * 4 + 3]),) + ((("t", rope_rot_t[hd - 2]),) if hd >= 2 else ())
                smk, vmk = split_hl(bank(hd % 2), qrah[:, qsl], qral[:, qsl],
                                    waits_s=ws, waits_v=ws)
                rope_raw_s.append(cnt["s"])
                rope_raw_vmk = vmk

                def _rv(vector, hd=hd, csq_sd=csq_sd):
                    wge(vector, "t", q_use[hd * 4 + 3])
                    if hd == 0:
                        wge(vector, "cs", csq_sd)
                    vector.tensor_tensor(qca[:, (hd % 2) * 512:(hd % 2) * 512 + 512],
                                         bank(hd % 2), costt[:], ALU.mult)
                    vector.drain()
                    inc(vector.last, "v")
                emit("vector", _rv)
                qc_v = cnt["v"]

                def _rt(tensor, hd=hd, qc_v=qc_v, rvm=rope_raw_vmk, qsl=qsl):
                    wge(tensor, "s", rope_raw_s[hd])
                    wge(tensor, "v", rvm)
                    wge(tensor, "v", qc_v)
                    if hd >= 2:
                        wge(tensor, "v", rope_done_v[hd - 2])
                    tensor.matmul(bank(2 + hd % 2), protB[:], qrah[:, qsl], start=True, stop=False)
                    tensor.matmul(bank(2 + hd % 2), protB[:], qral[:, qsl], start=False, stop=True)
                    inc(tensor.last, "t")
                emit("tensor", _rt)
                rope_rot_t.append(cnt["t"])

                def _rv2(vector, hd=hd):
                    wge(vector, "t", rope_rot_t[hd])
                    vector.tensor_tensor(qsa[:, (hd % 2) * 512:(hd % 2) * 512 + 512],
                                         bank(2 + hd % 2), sintt[:], ALU.mult)
                    vector.drain()
                    vector.tensor_tensor(qsa[:, (hd % 2) * 512:(hd % 2) * 512 + 512],
                                         qca[:, (hd % 2) * 512:(hd % 2) * 512 + 512],
                                         qsa[:, (hd % 2) * 512:(hd % 2) * 512 + 512], ALU.add)
                    vector.drain()
                    inc(vector.last, "v")
                emit("vector", _rv2)
                qr_v = cnt["v"]
                split_hl(qsa[:, (hd % 2) * 512:(hd % 2) * 512 + 512],
                         qrTh[:, hd * 512:(hd + 1) * 512],
                         qrTl[:, hd * 512:(hd + 1) * 512],
                         waits_s=(("v", qr_v),))
                rope_done_v.append(cnt["v"])


            for hd in range(H):
                for half in range(4):
                    n = hd * 4 + half

                    def _wql(sync, hd=hd, half=half, n=n, ft=cnt["t"], fv=cnt["v"], fs=cnt["s"],
                             wr=w_ready_all):
                        if n >= 2:
                            wge(sync, "t", q_use[n - 2])
                        else:
                            wge(sync, "t", ft)
                            wge(sync, "v", fv)
                            wge(sync, "s", fs)
                            wge(sync, "g", wr)
                        par = "wqa" if n % 2 == 0 else "wqb"
                        inc(sync.dma_start(out=wringqh[:, (n % 2) * 1024:(n % 2) * 1024 + 1024]
                                           .rearrange("p (a c) -> p a c", c=128),
                                           in_=wqh_sc[hd, half * 8:(half + 1) * 8].transpose([1, 0, 2])), par, 16)
                        inc(sync.dma_start(out=wringql[:, (n % 2) * 1024:(n % 2) * 1024 + 1024]
                                           .rearrange("p (a c) -> p a c", c=128),
                                           in_=wql_sc[hd, half * 8:(half + 1) * 8].transpose([1, 0, 2])), par, 16)
                    emit("sync", _wql)
                    wsd = cnt["wqa" if n % 2 == 0 else "wqb"]
                    wq_split_s.append(wsd)

                    sl_h = wringqh[:, (n % 2) * 1024:(n % 2) * 1024 + 1024]
                    sl_l = wringql[:, (n % 2) * 1024:(n % 2) * 1024 + 1024]

                    def _qmm(tensor, hd=hd, half=half, n=n, bc=banks_clear_s, bv=banks_clear_v,
                             sl_h=sl_h, sl_l=sl_l):
                        wge(tensor, "wqa" if n % 2 == 0 else "wqb", wq_split_s[n])
                        wge(tensor, "sp%d" % half, xq_marks[half])
                        if half == 0:
                            if hd == 0:
                                wge(tensor, "s", bc)
                                wge(tensor, "v", bv)
                            if hd >= 2:
                                wge(tensor, "v", rope_done_v[hd - 2])
                                wge(tensor, "s", rope_raw_s[hd - 2])
                        qb = bank(hd % 2)
                        for i2 in range(8):
                            ic = half * 8 + i2
                            mm3(tensor, qb,
                                sl_h[:, i2 * 128:(i2 + 1) * 128], sl_l[:, i2 * 128:(i2 + 1) * 128],
                                xT1th[:, ic * 512:(ic + 1) * 512], xT1tl[:, ic * 512:(ic + 1) * 512],
                                ic == 0, ic == NIC - 1)
                        inc(tensor.last, "t")
                    emit("tensor", _qmm)
                    q_use.append(cnt["t"])

                if hd >= 1:
                    emit_rope(hd - 1)

            emit_rope(H - 1)
            xT1t_free_cur = q_use[-1]
            csq_free = cnt["v"]
            rope_all_v = cnt["v"]
            rope_all_s = cnt["s"]
            st_a2.close()

            st_a3 = contextlib.ExitStack()
            sigf = st_a3.enter_context(nc.sbuf_tensor(f"sigf{t}", [128, 1536], F32))
            sigh = st_a3.enter_context(nc.sbuf_tensor(f"sigh{t}", [128, 1536], BF16))
            sigl = st_a3.enter_context(nc.sbuf_tensor(f"sigl{t}", [128, 1536], BF16))
            uall = st_a3.enter_context(nc.sbuf_tensor(f"uall{t}", [128, 8 * 516], F32))
            atm = st_a3.enter_context(nc.sbuf_tensor(f"atm{t}", [128, 1024], F32))
            btmp = st_a3.enter_context(nc.sbuf_tensor(f"btmp{t}", [128, 128], F32))
            bsm = st_a3.enter_context(nc.sbuf_tensor(f"bsm{t}", [128, 32], F32))
            st_fm = contextlib.ExitStack()
            afmh = st_fm.enter_context(nc.sbuf_tensor(f"afmh{t}", [128, NIC * 512], BF16, side="right"))
            afml = st_fm.enter_context(nc.sbuf_tensor(f"afml{t}", [128, NIC * 512], BF16, side="right"))

            sig_cons_t = 0
            sc_evac_s = [banks_clear_s, banks_clear_s]
            uev_s = banks_clear_s
            tr_ev_s = banks_clear_s
            tr_ev_v = banks_clear_v
            pending_uev = None
            trev_linked = False
            last_blend_v = 0
            last_tr_t = 0

            smm_marks = {}

            def emit_scores(qh2, h):
                qloc = 2 * t + qh2
                hg, hp = h // 8, h % 8
                pset = psA if h % 2 == 0 else psB
                def _smm2(tensor, h=h, qh2=qh2, qloc=qloc, pset=pset,
                         scv=sc_evac_s[h % 2], rav=rope_all_v, ras=rope_all_s, t=t):
                    if h < 2 and qh2 == 0:
                        wge(tensor, "v", rav)
                        wge(tensor, "s", ras)
                        if t == 0:
                            wge(tensor, "v", kv_ready_v)
                            wge(tensor, "s", kv_ready_s)
                    wge(tensor, "s", scv)
                    for i in range(6):
                        kc = 2 * qloc + i
                        has_mask = (t == 0 and qh2 == 0 and i < 2) or \
                                   (t == NT - 1 and qh2 == 1 and i >= 4)
                        q_h = qrTh[:, h * 512 + qh2 * 256: h * 512 + qh2 * 256 + 256]
                        q_l = qrTl[:, h * 512 + qh2 * 256: h * 512 + qh2 * 256 + 256]
                        mm3(tensor, pset[:, i * 256:(i + 1) * 256],
                            krTh[:, kc * 128:(kc + 1) * 128], krTl[:, kc * 128:(kc + 1) * 128],
                            q_h, q_l, True, not has_mask)
                        if has_mask:
                            mi = kc if kc < 2 else kc - 16
                            tensor.matmul(pset[:, i * 256:(i + 1) * 256],
                                      maskB[0:1, mi * 128:(mi + 1) * 128],
                                      onesB[0:1, 0:256], start=False, stop=True)
                    inc(tensor.last, "t")
                emit("tensor", _smm2)
                smm_marks[(qh2, h)] = cnt["t"]
            for qh2 in range(2):
                qloc = 2 * t + qh2
                for h in range(H):
                    hg, hp = h // 8, h % 8
                    pset = psA if h % 2 == 0 else psB

                    if (qh2, h) not in smm_marks:
                        emit_scores(qh2, h)
                    smm_t = smm_marks[(qh2, h)]

                    def _sgev(scalar, pset=pset, smm_t=smm_t, sct=sig_cons_t):
                        wge(scalar, "t", smm_t)
                        wge(scalar, "t", sct)
                        scalar.activation(sigf[:], pset[:, 0:1536], AF.Sigmoid, scale=SCALE)
                        scalar.drain()
                        inc(scalar.last, "s")
                    emit("scalar", _sgev)
                    sc_evac_s[h % 2] = cnt["s"]
                    sgev_s = cnt["s"]
                    if pending_uev is not None:
                        emit("scalar", pending_uev)
                        uev_s = cnt["s"]
                        pending_uev = None

                    def _slo(vector, sgev_s=sgev_s, sct=sig_cons_t):
                        wge(vector, "s", sgev_s)
                        wge(vector, "t", sct)
                        vector.tensor_scalar(sigh[:], sigf[:], 1.0, None, ALU.mult)
                        vector.drain()
                        vector.tensor_tensor(sigl[:], sigf[:], sigh[:], ALU.subtract)
                        vector.drain()
                        inc(vector.last, "v")
                    emit("vector", _slo)
                    slo_v = cnt["v"]
                    nxt = (qh2, h + 1) if h + 1 < H else ((1, 0) if qh2 == 0 else None)
                    if nxt is not None and nxt not in smm_marks:
                        emit_scores(*nxt)

                    def _umm(tensor, qloc=qloc, sgev_s=sgev_s, slo_v=slo_v, ue=uev_s,
                             tre=tr_ev_s, trev=tr_ev_v, lk=trev_linked):
                        wge(tensor, "v", slo_v)
                        wge(tensor, "s", ue)
                        if not lk:
                            wge(tensor, "s", tre)
                        wge(tensor, "v", trev)
                        slots = {(0, 0): psC[:, 0:129], (1, 0): psC[:, 129:258],
                                 (0, 1): psC[:, 258:387], (1, 1): psD[:, 0:129]}
                        for side in range(2):
                            kcs = [2 * qloc + 2 * side + j for j in range(4)]
                            for qc2 in range(2):
                                sl = slots[(side, qc2)]
                                for j, kc in enumerate(kcs):
                                    i = kc - 2 * qloc
                                    mm3(tensor, sl,
                                        sigh[:, i * 256 + qc2 * 128: i * 256 + qc2 * 128 + 128],
                                        sigl[:, i * 256 + qc2 * 128: i * 256 + qc2 * 128 + 128],
                                        vaugh[:, kc * 129:(kc + 1) * 129],
                                        vaugl[:, kc * 129:(kc + 1) * 129],
                                        j == 0, j == 3)
                        inc(tensor.last, "t")
                    emit("tensor", _umm)
                    sig_cons_t = cnt["t"]
                    umm_t = cnt["t"]

                    def _uev(scalar, hp=hp, umm_t=umm_t, lbv=last_blend_v):
                        wge(scalar, "t", umm_t)
                        if hp == 0:
                            wge(scalar, "v", lbv)
                        scalar.copy(uall[:, hp * 516: hp * 516 + 387], psC[:, 0:387])
                        scalar.drain()
                        scalar.copy(uall[:, hp * 516 + 387: hp * 516 + 516], psD[:, 0:129])
                        scalar.drain()
                        inc(scalar.last, "s")
                    if hp == 7:
                        emit("scalar", _uev)
                        uev_s = cnt["s"]
                    else:
                        pending_uev = _uev

                    if hp == 7:
                        for qc2 in range(2):
                            qcg = t * 4 + qh2 * 2 + qc2

                            def _bl(vector, qc2=qc2, qcg=qcg, ue=uev_s, ltt=last_tr_t):
                                wge(vector, "s", ue)
                                wge(vector, "t", ltt)
                                rl = bsm[:, 0:8]
                                rr = bsm[:, 8:16]
                                c1 = bsm[:, 16:24]
                                c2 = bsm[:, 24:32]
                                base_c = qc2 * 258
                                vector.tensor_copy(rl, uall[:, base_c + 128: base_c + 128 + 7 * 516 + 1: 516])
                                vector.drain()
                                vector.tensor_copy(rr, uall[:, base_c + 257: base_c + 257 + 7 * 516 + 1: 516])
                                vector.drain()
                                vector.reciprocal(rl, rl)
                                vector.drain()
                                vector.reciprocal(rr, rr)
                                vector.drain()
                                vector.tensor_scalar(c1, rl, alpmt[:, qcg:qcg + 1], None, ALU.mult)
                                vector.drain()
                                vector.tensor_scalar(c2, rr, alpt[:, qcg:qcg + 1], None, ALU.mult)
                                vector.drain()
                                for hp2 in range(8):
                                    if qc2 == 0:
                                        lsl = uall[:, hp2 * 516: hp2 * 516 + 128]
                                        rsl = uall[:, hp2 * 516 + 129: hp2 * 516 + 257]
                                    else:
                                        lsl = uall[:, hp2 * 516 + 258: hp2 * 516 + 386]
                                        rsl = uall[:, hp2 * 516 + 387: hp2 * 516 + 515]
                                    vector.tensor_scalar(btmp[:], rsl, c2[:, hp2:hp2 + 1], None, ALU.mult)
                                    vector.drain()
                                    vector.scalar_tensor_tensor(atm[:, hp2 * 128:(hp2 + 1) * 128],
                                                                lsl, c1[:, hp2:hp2 + 1], btmp[:],
                                                                ALU.mult, ALU.add)
                                    vector.drain()
                                inc(vector.last, "v")
                            emit("vector", _bl)
                            last_blend_v = cnt["v"]

                            def _tr(tensor, blm=last_blend_v, tre=tr_ev_s, trev=tr_ev_v, lk=trev_linked):
                                wge(tensor, "v", blm)
                                if not lk:
                                    wge(tensor, "s", tre)
                                wge(tensor, "v", trev)
                                for hp2 in range(8):
                                    dst = psC if hp2 < 4 else psD
                                    tensor.transpose(dst[:, (hp2 % 4) * 128:(hp2 % 4) * 128 + 128],
                                                     atm[:, hp2 * 128:(hp2 + 1) * 128], idf[:])
                                inc(tensor.last, "t")
                            emit("tensor", _tr)
                            last_tr_t = cnt["t"]

                            def _trevs(scalar, hg=hg, qh2=qh2, qc2=qc2, tr_t=last_tr_t):
                                wge(scalar, "t", tr_t)
                                qcol = (qh2 * 2 + qc2) * 128
                                a3h = afmh[:].rearrange("p (h q) -> p h q", q=512)
                                scalar.copy(a3h[:, hg * 8: hg * 8 + 4, qcol:qcol + 128],
                                            psC[:, 0:512].rearrange("p (a b) -> p a b", b=128))
                                scalar.drain()
                                scalar.copy(a3h[:, hg * 8 + 4: hg * 8 + 8, qcol:qcol + 128],
                                            psD[:, 0:512].rearrange("p (a b) -> p a b", b=128))
                                scalar.drain()
                                inc(scalar.last, "s")
                            emit("scalar", _trevs)
                            tr_ev_s = cnt["s"]

                            def _trevv(vector, hg=hg, qh2=qh2, qc2=qc2, tr_t=last_tr_t, ses=tr_ev_s):
                                wge(vector, "t", tr_t)
                                wge(vector, "s", ses)
                                qcol = (qh2 * 2 + qc2) * 128
                                a3h = afmh[:].rearrange("p (h q) -> p h q", q=512)
                                a3l = afml[:].rearrange("p (h q) -> p h q", q=512)
                                vector.tensor_tensor(a3l[:, hg * 8: hg * 8 + 4, qcol:qcol + 128],
                                                     psC[:, 0:512].rearrange("p (a b) -> p a b", b=128),
                                                     a3h[:, hg * 8: hg * 8 + 4, qcol:qcol + 128], ALU.subtract)
                                vector.drain()
                                vector.tensor_tensor(a3l[:, hg * 8 + 4: hg * 8 + 8, qcol:qcol + 128],
                                                     psD[:, 0:512].rearrange("p (a b) -> p a b", b=128),
                                                     a3h[:, hg * 8 + 4: hg * 8 + 8, qcol:qcol + 128], ALU.subtract)
                                vector.drain()
                                inc(vector.last, "v")
                            emit("vector", _trevv)
                            tr_ev_v = cnt["v"]
                            trev_linked = True
                            uev_s = tr_ev_s

            st_a3.close()
            st_q.close()

            st_a4 = contextlib.ExitStack()
            yb = st_a4.enter_context(nc.sbuf_tensor(f"yb{t}", [128, 2 * D], F32))
            wringoh = st_a4.enter_context(nc.sbuf_tensor(f"wringoh{t}", [128, 4 * 512], BF16))
            wringol = st_a4.enter_context(nc.sbuf_tensor(f"wringol{t}", [128, 4 * 512], BF16))
            xinp = st_a4.enter_context(nc.sbuf_tensor(f"xinp{t}", [128, D], F32))
            resp = st_a4.enter_context(nc.sbuf_tensor(f"resp{t}", [128, D], F32))
            x2p = xinp
            x2tph = st_a4.enter_context(nc.sbuf_tensor(f"x2tph{t}", [128, D], BF16))
            x2tpl = st_a4.enter_context(nc.sbuf_tensor(f"x2tpl{t}", [128, D], BF16))

            wo_use = []
            wo_split_s = []
            wo_split_v = []
            attn_clear_s = cnt["s"]
            attn_clear_v = cnt["v"]
            prev_x2t_s = 0
            prev_x2t_v = 0

            for halfq in range(2):
                y_ev_s = [0] * 8
                for og in range(8):
                    for ic in range(NIC):
                        ni = halfq * 8 * NIC + og * NIC + ic

                        def _wol(sync, og=og, ic=ic, ni=ni, ft=cnt["t"], fv=cnt["v"], fs=cnt["s"],
                                 wr=w_ready_all):
                            if ni >= 2:
                                wge(sync, "t", wo_use[ni - 2])
                            else:
                                wge(sync, "t", ft)
                                wge(sync, "v", fv)
                                wge(sync, "s", fs)
                                wge(sync, "g", wr)
                            par = "woa" if ni % 2 == 0 else "wob"
                            inc(sync.dma_start(out=wringoh[:, (ni % 2) * 512:(ni % 2) * 512 + 512],
                                               in_=woh_sc[og, ic]), par, 16)
                            inc(sync.dma_start(out=wringol[:, (ni % 2) * 512:(ni % 2) * 512 + 512],
                                               in_=wol_sc[og, ic]), par, 16)
                        emit("sync", _wol)
                        wsd = cnt["woa" if ni % 2 == 0 else "wob"]
                        wo_split_s.append(wsd)

                        sl_h = wringoh[:, (ni % 2) * 512:(ni % 2) * 512 + 512]
                        sl_l = wringol[:, (ni % 2) * 512:(ni % 2) * 512 + 512]

                        def _wom(tensor, halfq=halfq, og=og, ic=ic, ni=ni,
                                 acs=attn_clear_s, acv=attn_clear_v, yfv=y_free_prev_v,
                                 sl_h=sl_h, sl_l=sl_l):
                            wge(tensor, "woa" if ni % 2 == 0 else "wob", wo_split_s[ni])
                            if ni == 0:
                                wge(tensor, "s", acs)
                                wge(tensor, "v", acv)
                                wge(tensor, "v", yfv)
                            if ic == 0 and og >= 2:
                                wge(tensor, "s", y_ev_s[og - 2])
                            pb = [0, 1] if og % 2 == 0 else [2, 3]
                            for tc2 in range(2):
                                tc = halfq * 2 + tc2
                                mm3(tensor, bank(pb[tc2]),
                                    afmh[:, ic * 512 + tc * 128: ic * 512 + tc * 128 + 128],
                                    afml[:, ic * 512 + tc * 128: ic * 512 + tc * 128 + 128],
                                    sl_h, sl_l, ic == 0, ic == NIC - 1)
                            inc(tensor.last, "t")
                        emit("tensor", _wom)
                        wo_use.append(cnt["t"])

                    def _yev(scalar, og=og, mk=wo_use[-1]):
                        wge(scalar, "t", mk)
                        pb = [0, 1] if og % 2 == 0 else [2, 3]
                        for tc2 in range(2):
                            scalar.copy(yb[:, tc2 * D + og * 512: tc2 * D + og * 512 + 512], bank(pb[tc2]))
                            scalar.drain()
                        inc(scalar.last, "s")
                    emit("scalar", _yev)
                    y_ev_s[og] = cnt["s"]

                wo_done_s = cnt["s"]

                for tc2 in range(2):
                    tc = halfq * 2 + tc2
                    ch = t * 4 + tc

                    def _ldxin(g, t=t, tc=tc, pv=cnt["v"], fs=cnt["s"], px=prev_x2t_s):
                        wge(g, "v", pv)
                        wge(g, "s", fs if px == 0 else px)
                        inc(g.dma_start(out=xinp[:],
                                        in_=xs[256 + t * 512 + tc * 128: 256 + t * 512 + (tc + 1) * 128, :]), "gd", 16)
                        gflush(g)
                    emit("gpsimd", _ldxin)
                    xin_gd = cnt["g"]

                    lnw = [("s", wo_done_s)]
                    if ch > 0:
                        lnw.append(("g", res1_gd[ch - 1]))
                    ln_chunk(yb[:, tc2 * D:(tc2 + 1) * D], resp[:], base=8, waits=tuple(lnw))

                    def _addx(vector, xin_gd=xin_gd):
                        wge(vector, "g", xin_gd)
                        vector.tensor_tensor(resp[:], resp[:], xinp[:], ALU.add)
                        vector.drain()
                        inc(vector.last, "v")
                    emit("vector", _addx)
                    res_v = cnt["v"]

                    def _spillr(g, ch=ch, res_v=res_v):
                        wge(g, "v", res_v)
                        inc(g.dma_start(out=res1_sc[ch], in_=resp[:]), "gd", 16)
                        gflush(g)
                    emit("gpsimd", _spillr)
                    res1_gd[ch] = cnt["g"]

                    ln2m = ln_chunk(resp[:], x2p[:], base=16, waits=(("v", res_v),))

                    srcs = [x2p[:, ic * 128:(ic + 1) * 128] for ic in range(NIC)]
                    dsth = [x2tph[:, ic * 128:(ic + 1) * 128] for ic in range(NIC)]
                    dstl = [x2tpl[:, ic * 128:(ic + 1) * 128] for ic in range(NIC)]
                    fw = [("v", ln2m)]
                    if tc == 0 and t > 0:
                        fw.append(("g", x2tp_prev_gd))
                    x2t_s, x2t_v = transpose_split(srcs, dsth, dstl, first_waits=tuple(fw), psum_off=4)
                    prev_x2t_s = x2t_s
                    prev_x2t_v = x2t_v

                    def _spillx2(g, t=t, tc=tc, x2t_s=x2t_s, x2t_v=x2t_v):
                        wge(g, "s", x2t_s)
                        wge(g, "v", x2t_v)
                        for ic in range(NIC):
                            inc(g.dma_start(out=x2Th_sc[t, ic, :, tc * 128:(tc + 1) * 128],
                                            in_=x2tph[:, ic * 128:(ic + 1) * 128]), "gd", 16)
                            inc(g.dma_start(out=x2Tl_sc[t, ic, :, tc * 128:(tc + 1) * 128],
                                            in_=x2tpl[:, ic * 128:(ic + 1) * 128]), "gd", 16)
                        gflush(g)
                    emit("gpsimd", _spillx2)
                    x2tp_prev_gd = cnt["g"]
                    if tc == 3:
                        x2T_gd[t] = cnt["g"]

                    def _rred(vector, t=t, tc=tc, x2t_s=x2t_s):
                        wge(vector, "s", x2t_s)
                        for ic in range(NIC):
                            vector.reduce_sum(smalls[:, 68:69], x2tph[:, ic * 128:(ic + 1) * 128], axis=AX.X)
                            vector.drain()
                            vector.reduce_sum(smalls[:, 69:70], x2tpl[:, ic * 128:(ic + 1) * 128], axis=AX.X)
                            vector.drain()
                            vector.tensor_tensor(zbuf[:, ic * 16 + t * 4 + tc: ic * 16 + t * 4 + tc + 1],
                                                 smalls[:, 68:69], smalls[:, 69:70], ALU.add)
                            vector.drain()
                        inc(vector.last, "v")
                    emit("vector", _rred)
                    x2tp_free_v = cnt["v"]

                    def _guard(scalar, vmk=x2tp_free_v):
                        wge(scalar, "v", vmk)
                        inc(scalar.copy(smalls[:, 67:68], cct[:, 2:3]), "s")
                    emit("scalar", _guard)

            route_red_v = cnt["v"]
            y_free_prev_v = cnt["v"]
            xT1t_free_prev = xT1t_free_cur
            st_a4.close()
            st_fm.close()

        if STAGE < 5:
            def _fini0(g):
                g.wait_ge(sems["gd"], cnt["gd"])
                g.wait_ge(sems["v"], cnt["v"])
                g.wait_ge(sems["s"], cnt["s"])
                g.wait_ge(sems["t"], cnt["t"])
            emit("gpsimd", _fini0)
            return nc
        router_sclear = cnt["s"]

        def _z32(vector):
            wge(vector, "v", route_red_v)
            vector.reduce_sum(z32[:], zbuf[:].rearrange("p (ic k) -> p ic k", k=16), axis=AX.X)
            vector.drain()
            inc(vector.last, "v")
        emit("vector", _z32)
        z32_v = cnt["v"]
        split_hl(z32[:], z32h[:], z32l[:], waits_s=(("v", z32_v),))
        z32s_v = cnt["v"]
        z32s_s = cnt["s"]

        def _rmm(tensor):
            wge(tensor, "v", z32s_v)
            wge(tensor, "s", z32s_s)
            wge(tensor, "s", router_sclear)
            for ic in range(NIC):
                mm3(tensor, psC[0:1, 0:NE],
                    z32h[:, ic:ic + 1], z32l[:, ic:ic + 1],
                    kaph[:, ic * NE:(ic + 1) * NE], kapl[:, ic * NE:(ic + 1) * NE],
                    ic == 0, ic == NIC - 1)
            inc(tensor.last, "t")
        emit("tensor", _rmm)
        rmm_t = cnt["t"]

        def _rev(scalar):
            wge(scalar, "t", rmm_t)
            scalar.copy(routs[:], psC[0:1, 0:NE])
            scalar.drain()
            inc(scalar.last, "s")
        emit("scalar", _rev)
        rev_s = cnt["s"]
        smk, vmk = split_hl(routs[:], routsh[:], routsl[:], waits_s=(("s", rev_s),))
        revh_s, revh_v = smk, vmk

        def _rbm(tensor):
            wge(tensor, "s", revh_s)
            wge(tensor, "v", revh_v)
            tensor.matmul(psD[0:NB, 0:NE], bmtB[:], routsh[:], start=True, stop=False)
            tensor.matmul(psD[0:NB, 0:NE], bmtB[:], routsl[:], start=False, stop=True)
            inc(tensor.last, "t")
        emit("tensor", _rbm)
        rbm_t = cnt["t"]

        def _rev2(scalar):
            wge(scalar, "t", rbm_t)
            scalar.copy(routp[:], psD[0:NB, 0:NE])
            scalar.drain()
            inc(scalar.last, "s")
        emit("scalar", _rev2)
        rev2_s = cnt["s"]

        def _cc(g):
            wge(g, "s", rev2_s)
            inc(g.dma_start(out=cc_in[:], in_=routp[:]), "gd", 16)
            g.wait_ge(sems["gd"], cnt["gd"])
            inc(g.collective_compute("AllReduce", ALU.add,
                                     replica_groups=[list(range(NCORE))],
                                     ins=[cc_in[:].opt()], outs=[cc_out[:].opt()]), "cc", 1)
            g.wait_ge(sems["cc"], cnt["cc"])
            inc(g.dma_start(out=rout2[:], in_=cc_out[:]), "gd", 16)
            gflush(g)
        emit("gpsimd", _cc)
        cc_gd = cnt["g"]

        smk, vmk = split_hl(rout2[:], rout2h[:], rout2l[:],
                            waits_s=(("g", cc_gd),), waits_v=(("g", cc_gd),))
        r2h_s, r2h_v = smk, vmk

        def _rsel(tensor):
            wge(tensor, "s", r2h_s)
            wge(tensor, "v", r2h_v)
            tensor.matmul(psC[0:1, 0:NE], bmpB[:], rout2h[:], start=True, stop=False)
            tensor.matmul(psC[0:1, 0:NE], bmpB[:], rout2l[:], start=False, stop=True)
            inc(tensor.last, "t")
        emit("tensor", _rsel)
        rsel_t = cnt["t"]

        def _rsm(scalar):
            wge(scalar, "t", rsel_t)
            scalar.copy(routs[:], psC[0:1, 0:NE])
            scalar.drain()
            inc(scalar.last, "s")
        emit("scalar", _rsm)
        rsm_s = cnt["s"]

        def _smx(vector):
            wge(vector, "s", rsm_s)
            vector.reduce_max(smalls[0:1, 70:71], routs[:], axis=AX.X)
            vector.drain()
            vector.tensor_scalar(routs[:], routs[:], smalls[0:1, 70:71], None, ALU.subtract)
            vector.drain()
            inc(vector.last, "v")
        emit("vector", _smx)
        smx_v = cnt["v"]

        def _sexp(scalar):
            wge(scalar, "v", smx_v)
            scalar.activation(routs[:], routs[:], AF.Exp, scale=ROUTE_C)
            scalar.drain()
            inc(scalar.last, "s")
        emit("scalar", _sexp)
        sexp_s = cnt["s"]

        def _snrm(vector):
            wge(vector, "s", sexp_s)
            vector.reduce_sum(smalls[0:1, 70:71], routs[:], axis=AX.X)
            vector.drain()
            vector.reciprocal(smalls[0:1, 71:72], smalls[0:1, 70:71])
            vector.drain()
            vector.tensor_scalar(coefs[:], routs[:], smalls[0:1, 71:72], None, ALU.mult)
            vector.drain()
            inc(vector.last, "v")
        emit("vector", _snrm)
        coef_v = cnt["v"]

        smk, vmk = split_hl(coefs[:], coefsh[:], coefsl[:], waits_s=(("v", coef_v),))
        cfh_s, cfh_v = smk, vmk

        def _cbc(tensor):
            wge(tensor, "s", cfh_s)
            wge(tensor, "v", cfh_v)
            tensor.matmul(psC[:, 0:NE], onesB[0:1, 0:128], coefsh[:], start=True, stop=False)
            tensor.matmul(psC[:, 0:NE], onesB[0:1, 0:128], coefsl[:], start=False, stop=True)
            inc(tensor.last, "t")
        emit("tensor", _cbc)
        cbc_t = cnt["t"]

        def _cbev(scalar):
            wge(scalar, "t", cbc_t)
            scalar.copy(bc8[:], psC[:, 0:NE])
            scalar.drain()
            inc(scalar.last, "s")
        emit("scalar", _cbev)
        cbev_s = cnt["s"]

        def _cerf(vector):
            wge(vector, "s", cbev_s)
            vector.tensor_copy(cer[0:64, :], bc8[0:64, 0:8:2])
            vector.drain()
            vector.tensor_copy(cer[64:128, :], bc8[64:128, 1:8:2])
            vector.drain()
            inc(vector.last, "v")
        emit("vector", _cerf)
        cer_v = cnt["v"]

        if STAGE < 6:
            def _fini0(g):
                g.wait_ge(sems["gd"], cnt["gd"])
                g.wait_ge(sems["v"], cnt["v"])
                g.wait_ge(sems["s"], cnt["s"])
                g.wait_ge(sems["t"], cnt["t"])
            emit("gpsimd", _fini0)
            return nc
        x2tt_free = 0
        b_clear_s = cbev_s
        prev_out_gd = 0
        prev_stg_gd = 0
        for t in range(NT):
            st_b = contextlib.ExitStack()
            x2tth = st_b.enter_context(nc.sbuf_tensor(f"x2tth{t}", [128, NIC * 512], BF16))
            x2ttl = st_b.enter_context(nc.sbuf_tensor(f"x2ttl{t}", [128, NIC * 512], BF16))
            wringbh = st_b.enter_context(nc.sbuf_tensor(f"wringbh{t}", [128, 4 * 512], BF16))
            wringbl = st_b.enter_context(nc.sbuf_tensor(f"wringbl{t}", [128, 4 * 512], BF16))
            ldnrh = st_b.enter_context(nc.sbuf_tensor(f"ldnrh{t}", [128, 2 * 2048], BF16))
            ldnrl = st_b.enter_context(nc.sbuf_tensor(f"ldnrl{t}", [128, 2 * 2048], BF16))
            hsf = st_b.enter_context(nc.sbuf_tensor(f"hsf{t}", [128, 512], F32))
            hsh = st_b.enter_context(nc.sbuf_tensor(f"hsh{t}", [128, 4 * 512], BF16))
            hsl = st_b.enter_context(nc.sbuf_tensor(f"hsl{t}", [128, 4 * 512], BF16))
            stg = st_b.enter_context(nc.sbuf_tensor(f"stg{t}", [128, 4 * 512], F32))
            luprh = st_b.enter_context(nc.sbuf_tensor(f"luprh{t}", [128, 2 * 512], BF16))
            luprl = st_b.enter_context(nc.sbuf_tensor(f"luprl{t}", [128, 2 * 512], BF16))
            ffb = st_b.enter_context(nc.sbuf_tensor(f"ffb{t}", [128, D], F32, side="right"))
            rrb = st_b.enter_context(nc.sbuf_tensor(f"rrb{t}", [128, D], F32, side="right"))

            x2q_marks = [0, 0, 0, 0]

            def _ldx2(g, t=t, xf=x2tt_free, fv=cnt["v"], fs=cnt["s"], pg=cnt["gd"],
                      sp_now=tuple(cnt["sp%d" % i] for i in range(4))):
                wge(g, "t", xf)
                wge(g, "gd", pg)
                wge(g, "v", fv)
                wge(g, "s", fs)
                for i in range(4):
                    wge(g, "sp%d" % i, sp_now[i])
                for q in range(4):
                    spn = "sp%d" % q
                    for ic in range(q * 8, (q + 1) * 8):
                        inc(g.dma_start(out=x2tth[:, ic * 512:(ic + 1) * 512], in_=x2Th_sc[t, ic]), spn, 16)
                        inc(g.dma_start(out=x2ttl[:, ic * 512:(ic + 1) * 512], in_=x2Tl_sc[t, ic]), spn, 16)
                    x2q_marks[q] = cnt[spn]
            emit("gpsimd", _ldx2)

            h_use = []
            hev_v = []
            for er in range(4):
                for half in range(2):
                    nn = er * 2 + half

                    def _ldh(sync, er=er, half=half, nn=nn, ft=cnt["t"], fv=cnt["v"], fs=cnt["s"],
                             wr=w_ready_all):
                        if nn >= 2:
                            wge(sync, "t", h_use[nn - 2])
                        else:
                            wge(sync, "t", ft)
                            wge(sync, "v", fv)
                            wge(sync, "s", fs)
                            wge(sync, "g", wr)
                        par = "lda" if nn % 2 == 0 else "ldb"
                        inc(sync.dma_start(out=ldnrh[:, (nn % 2) * 2048:(nn % 2) * 2048 + 2048]
                                           .rearrange("p (a c) -> p a c", c=128),
                                           in_=ldh_sc[er, half * 16:(half + 1) * 16].transpose([1, 0, 2])), par, 16)
                        inc(sync.dma_start(out=ldnrl[:, (nn % 2) * 2048:(nn % 2) * 2048 + 2048]
                                           .rearrange("p (a c) -> p a c", c=128),
                                           in_=ldl_sc[er, half * 16:(half + 1) * 16].transpose([1, 0, 2])), par, 16)
                    emit("sync", _ldh)
                    ldh_sd = cnt["lda" if nn % 2 == 0 else "ldb"]

                    sl_h = ldnrh[:, (nn % 2) * 2048:(nn % 2) * 2048 + 2048]
                    sl_l = ldnrl[:, (nn % 2) * 2048:(nn % 2) * 2048 + 2048]

                    def _hmm(tensor, er=er, half=half, nn=nn, bc=b_clear_s,
                             sl_h=sl_h, sl_l=sl_l, ldh_sd=ldh_sd):
                        wge(tensor, "lda" if nn % 2 == 0 else "ldb", ldh_sd)
                        wge(tensor, "sp%d" % (half * 2), x2q_marks[half * 2])
                        wge(tensor, "sp%d" % (half * 2 + 1), x2q_marks[half * 2 + 1])
                        if nn == 0:
                            wge(tensor, "s", bc)
                        if half == 0 and er >= 2:
                            wge(tensor, "v", hev_v[er - 2])
                        for i2 in range(16):
                            ic = half * 16 + i2
                            mm3(tensor, bank(er),
                                sl_h[:, i2 * 128:(i2 + 1) * 128], sl_l[:, i2 * 128:(i2 + 1) * 128],
                                x2tth[:, ic * 512:(ic + 1) * 512], x2ttl[:, ic * 512:(ic + 1) * 512],
                                ic == 0, ic == NIC - 1)
                        inc(tensor.last, "t")
                    emit("tensor", _hmm)
                    h_use.append(cnt["t"])

                def _hev(vector, er=er, mk=h_use[-1], cv=cer_v, ps=cnt["s"]):
                    wge(vector, "t", mk)
                    wge(vector, "v", cv)
                    wge(vector, "s", ps)
                    vector.tensor_scalar(hsf[:], bank(er), cer[:, er:er + 1], None, ALU.mult)
                    vector.drain()
                    inc(vector.last, "v")
                emit("vector", _hev)
                hv = cnt["v"]
                split_hl(hsf[:],
                         hsh[:, er * 512:(er + 1) * 512],
                         hsl[:, er * 512:(er + 1) * 512],
                         waits_s=(("v", hv),))
                hev_v.append(cnt["v"])
            hs_done_v = cnt["v"]
            hs_done_s = cnt["s"]

            wb_use = []
            upm_marks = []
            ffn_first_s = [0] * 8
            stg_gd = [prev_stg_gd] * 8
            lup_splits = []

            for og in range(8):
                for ic in range(NIC):
                    ni = og * NIC + ic

                    def _wbl(sync, og=og, ic=ic, ni=ni, ft=cnt["t"], fv=cnt["v"], fs=cnt["s"],
                             wr=w_ready_all):
                        if ni >= 2:
                            wge(sync, "t", wb_use[ni - 2])
                        else:
                            wge(sync, "t", ft)
                            wge(sync, "v", fv)
                            wge(sync, "s", fs)
                            wge(sync, "g", wr)
                        par = "wba" if ni % 2 == 0 else "wbb"
                        inc(sync.dma_start(out=wringbh[:, (ni % 2) * 512:(ni % 2) * 512 + 512],
                                           in_=wbh_sc[og, ic]), par, 16)
                        inc(sync.dma_start(out=wringbl[:, (ni % 2) * 512:(ni % 2) * 512 + 512],
                                           in_=wbl_sc[og, ic]), par, 16)
                    emit("sync", _wbl)
                    wsd = cnt["wba" if ni % 2 == 0 else "wbb"]

                    sl_h = wringbh[:, (ni % 2) * 512:(ni % 2) * 512 + 512]
                    sl_l = wringbl[:, (ni % 2) * 512:(ni % 2) * 512 + 512]

                    def _wbm(tensor, og=og, ic=ic, ni=ni, hv=hs_done_v, hs=hs_done_s,
                             sl_h=sl_h, sl_l=sl_l, wsd=wsd):
                        wge(tensor, "wba" if ni % 2 == 0 else "wbb", wsd)
                        if ni == 0:
                            wge(tensor, "v", hv)
                            wge(tensor, "s", hs)
                        if ic == 0 and og >= 2:
                            wge(tensor, "s", ffn_first_s[og - 2])
                        pb = [0, 1, 2, 3] if og % 2 == 0 else [4, 5, 6, 7]
                        for tc in range(4):
                            mm3(tensor, bank(pb[tc]),
                                x2tth[:, ic * 512 + tc * 128: ic * 512 + tc * 128 + 128],
                                x2ttl[:, ic * 512 + tc * 128: ic * 512 + tc * 128 + 128],
                                sl_h, sl_l, ic == 0, False)
                        inc(tensor.last, "t")
                    emit("tensor", _wbm)
                    wb_use.append(cnt["t"])

                for er in range(4):
                    un = og * 4 + er

                    def _lupl(sync, og=og, er=er, un=un, ft=cnt["t"], fv=cnt["v"], fs=cnt["s"],
                              wr=w_ready_all):
                        if un >= 2:
                            wge(sync, "t", lup_splits[un - 2][2])
                        else:
                            wge(sync, "t", ft)
                            wge(sync, "v", fv)
                            wge(sync, "s", fs)
                            wge(sync, "g", wr)
                        par = "lua" if un % 2 == 0 else "lub"
                        inc(sync.dma_start(out=luprh[:, (un % 2) * 512:(un % 2) * 512 + 512],
                                           in_=luh_sc[og, er]), par, 16)
                        inc(sync.dma_start(out=luprl[:, (un % 2) * 512:(un % 2) * 512 + 512],
                                           in_=lul_sc[og, er]), par, 16)
                    emit("sync", _lupl)
                    usd = cnt["lua" if un % 2 == 0 else "lub"]

                    usl_h = luprh[:, (un % 2) * 512:(un % 2) * 512 + 512]
                    usl_l = luprl[:, (un % 2) * 512:(un % 2) * 512 + 512]

                    def _upm(tensor, og=og, er=er, un=un, usd=usd,
                             usl_h=usl_h, usl_l=usl_l):
                        wge(tensor, "lua" if un % 2 == 0 else "lub", usd)
                        pb = [0, 1, 2, 3] if og % 2 == 0 else [4, 5, 6, 7]
                        for tc in range(4):
                            mm3(tensor, bank(pb[tc]),
                                hsh[:, er * 512 + tc * 128: er * 512 + tc * 128 + 128],
                                hsl[:, er * 512 + tc * 128: er * 512 + tc * 128 + 128],
                                usl_h, usl_l, False, er == 3)
                        inc(tensor.last, "t")
                    emit("tensor", _upm)
                    lup_splits.append((smk, vmk, cnt["t"]))
                upm_marks.append(cnt["t"])

                def _fev(scalar, og=og, upm_t=upm_marks[-1], sgd=stg_gd[og - 1] if og > 0 else stg_gd[7]):
                    wge(scalar, "t", upm_t)
                    wge(scalar, "g", sgd)
                    pb = [0, 1, 2, 3] if og % 2 == 0 else [4, 5, 6, 7]
                    for tc in range(4):
                        scalar.copy(stg[:, tc * 512:(tc + 1) * 512], bank(pb[tc]))
                        scalar.drain()
                    inc(scalar.last, "s")
                emit("scalar", _fev)
                ffn_first_s[og] = cnt["s"]

                def _fsp(g, t=t, og=og, mk=ffn_first_s[og]):
                    wge(g, "s", mk)
                    for tc in range(4):
                        inc(g.dma_start(out=ffn_sc[t * 4 + tc, :, og * 512:(og + 1) * 512],
                                        in_=stg[:, tc * 512:(tc + 1) * 512]), "gd", 16)
                    gflush(g)
                emit("gpsimd", _fsp)
                stg_gd[og] = cnt["g"]

            prev_stg_gd = stg_gd[7]
            x2tt_free = cnt["t"]
            ffn_done_gd = cnt["gd"]
            b_clear_s = cnt["s"]

            for tc in range(4):
                ch = t * 4 + tc

                def _ldf(g, ch=ch, pv=cnt["v"], pg=cnt["gd"]):
                    wge(g, "v", pv)
                    wge(g, "gd", pg)
                    inc(g.dma_start(out=ffb[:], in_=ffn_sc[ch]), "gd", 16)
                    inc(g.dma_start(out=rrb[:], in_=res1_sc[ch]), "gd", 16)
                    gflush(g)
                emit("gpsimd", _ldf)
                ldf_gd = cnt["g"]

                lnf = ln_chunk(ffb[:], ffb[:], base=24, waits=(("g", ldf_gd),))

                def _fin(vector):
                    vector.tensor_tensor(ffb[:], ffb[:], rrb[:], ALU.add)
                    vector.drain()
                    inc(vector.last, "v")
                emit("vector", _fin)
                fin_v = cnt["v"]

                def _outd(g, t=t, tc=tc, fin_v=fin_v):
                    wge(g, "v", fin_v)
                    inc(g.dma_start(out=out[t * 512 + tc * 128: t * 512 + (tc + 1) * 128, :], in_=ffb[:]), "gd", 16)
                emit("gpsimd", _outd)
                prev_out_gd = cnt["gd"]

            st_b.close()

        def _fini(g):
            g.wait_ge(sems["gd"], cnt["gd"])
        emit("gpsimd", _fini)

    return nc



def _rope_cache():
    inv_freq = 1.0 / 10000.0 ** (np.arange(0, DH, 2, dtype=np.float32) / DH)
    t = np.arange(S, dtype=np.float32)
    f = np.outer(t, inv_freq)
    cos = np.zeros((S, DH), dtype=np.float32)
    sin = np.zeros((S, DH), dtype=np.float32)
    cos[:, 0::2] = np.cos(f)
    cos[:, 1::2] = np.cos(f)
    sin[:, 0::2] = np.sin(f)
    sin[:, 1::2] = np.sin(f)
    return cos, sin


def _stage_inputs(L0_attn_x, L0_attn_Wq, L0_attn_Wk, L0_attn_Wv, L0_attn_Wo,
                  L0_ffn_W_org, L0_ffn_lora_down, L0_ffn_lora_up, L0_ffn_lora_keys, L0_ffn_Wq):
    f32 = np.float32
    cos, sin = _rope_cache()

    wqT = np.ascontiguousarray(np.asarray(L0_attn_Wq).T)
    woT = np.ascontiguousarray(np.asarray(L0_attn_Wo).T)
    worgT = np.ascontiguousarray(np.asarray(L0_ffn_W_org).T)
    wkT = np.ascontiguousarray(np.asarray(L0_attn_Wk).T)
    wvT = np.ascontiguousarray(np.asarray(L0_attn_Wv).T)

    wq_t = np.ascontiguousarray(wqT.reshape(NIC, 128, H, 128).transpose(2, 0, 1, 3))
    wo_t = np.ascontiguousarray(woT.reshape(NIC, 128, 8, 512).transpose(2, 0, 1, 3))
    worg_t = np.ascontiguousarray(worgT.reshape(NIC, 128, 8, 512).transpose(2, 0, 1, 3))
    wkv_t = np.ascontiguousarray(np.concatenate(
        [wkT.reshape(NIC, 128, 128), wvT.reshape(NIC, 128, 128)], axis=2))

    ld = np.asarray(L0_ffn_lora_down)
    ldn_full = np.concatenate([ld[e] for e in range(NE)], axis=1)
    ldn_t = np.ascontiguousarray(ldn_full.reshape(NIC, 128, 4, 128).transpose(2, 0, 1, 3))
    lu = np.asarray(L0_ffn_lora_up).reshape(NE * RR, D)
    lup_t = np.ascontiguousarray(lu.reshape(4, 128, 8, 512).transpose(2, 0, 1, 3))

    prot = np.zeros((128, 128), f32)
    for i in range(64):
        prot[i, i + 64] = -1.0
    for i in range(64, 128):
        prot[i, i - 64] = 1.0
    protT = np.ascontiguousarray(prot.T)

    ident = np.eye(128, dtype=f32)
    ones_r = np.ones((1, 512), f32)
    constc = np.zeros((128, 4), f32)
    constc[:, 0] = EPS
    constc[:, 1] = 1.0
    constc[:, 2] = 0.0
    constc[:, 3] = 1.0 / D

    x_full = np.asarray(L0_attn_x)

    in_maps = []
    for c in range(NCORE):
        b = c // 4
        q0 = (c % 4) * CORE_T
        lo, hi = q0 - HALO, q0 + CORE_T + HALO
        shard = np.zeros((SH_R, D), f32)
        slo, shi = max(lo, 0), min(hi, S)
        shard[slo - lo: shi - lo] = x_full[b, slo:shi]

        pos = np.clip(np.arange(lo, hi), 0, S - 1)
        cosT_s = np.ascontiguousarray(cos[pos].T)
        sinT_s = np.ascontiguousarray(sin[pos].T)

        maskr = np.zeros((1, 512), f32)
        if lo < 0:
            maskr[0, 0:256] = MASK_NEG
        if hi > S:
            maskr[0, 256:512] = MASK_NEG

        alp = np.zeros((128, 16), f32)
        for j in range(16):
            offs = (q0 + 128 * j + np.arange(128)) % 256
            alp[:, j] = offs / 255.0
            if (q0 + 128 * j) // 256 == 0:
                alp[:, j] = 1.0
        alpm = 1.0 - alp

        bm_t = np.zeros((1, NB), f32)
        bm_t[0, b] = 1.0
        bm_p = np.zeros((NB, 1), f32)
        bm_p[b, 0] = 1.0

        in_maps.append({
            "xs": shard, "wq": wq_t, "wo": wo_t, "worg": worg_t, "wkv": wkv_t,
            "ldn": ldn_t, "lup": lup_t, "wqf": np.ascontiguousarray(np.asarray(L0_ffn_Wq)),
            "keysT": np.ascontiguousarray(np.asarray(L0_ffn_lora_keys).T),
            "cosT": cosT_s, "sinT": sinT_s, "protI": protT, "identf": ident,
            "alpha_c": alp, "alpham_c": alpm, "mask_r": maskr, "ones_r": ones_r,
            "bmaskT": bm_t, "bmaskP": bm_p, "constc": constc,
        })
    return in_maps


def kernel(**inputs):
    in_maps = _stage_inputs(**inputs)
    nc = build_program()
    ncore_run = int(os.environ.get("KNCORE", str(NCORE)))
    res = run_bass_kernel_spmd(nc, in_maps[:ncore_run], core_ids=list(range(ncore_run)))
    if ncore_run < NCORE:
        full = np.zeros((NB, S, D), np.float32)
        for c in range(ncore_run):
            full[c // 4, (c % 4) * CORE_T:(c % 4) * CORE_T + CORE_T] = res.results[c]["out"]
        return full
    outs = [res.results[c]["out"] for c in range(NCORE)]
    full = np.empty((NB, S, D), np.float32)
    for c in range(NCORE):
        b = c // 4
        q0 = (c % 4) * CORE_T
        full[b, q0:q0 + CORE_T] = outs[c]
    return full
